# version 16
# baseline (speedup 1.0000x reference)
import contextlib
import os
import numpy as np
import ml_dtypes
import concourse.bass as bass
import concourse.mybir as mybir
from concourse.bass_utils import run_bass_kernel_spmd

F32 = mybir.dt.float32
BF16 = mybir.dt.bfloat16
AF = mybir.ActivationFunctionType
OP = mybir.AluOpType
AX = mybir.AxisListType

L = 4096
C = 1024
H = 16
D = 64
DFF = 4096
NP = 8
NDC = 8
NFC = 32
TQ = 512
NTC = L // TQ
EPS_LN = 1e-5
EPS_RMS = 1e-8
SCALE = 0.125
BMARG = 1.02

ENGS = ("sync", "tensor", "scalar", "vector", "gpsimd")


class Sem:
    def __init__(self, h):
        self.h = h
        self.n = 0

    def at(self):
        return (self, self.n)


def build_nc(introspect=False):
    nc = bass.Bass("TRN2", target_bir_lowering=False, debug=False,
                   detect_race_conditions=not os.environ.get("KSIMNORD"))
    dt = nc.dram_tensor
    x = dt("x", [L, C], F32, kind="ExternalInput").ap()
    wqkv = dt("wqkv", [C, 3 * C], F32, kind="ExternalInput").ap()
    wproj = dt("wproj", [C, C], F32, kind="ExternalInput").ap()
    bprojT = dt("bprojT", [128, NDC], F32, kind="ExternalInput").ap()
    wup = dt("wup", [DFF, C], F32, kind="ExternalInput").ap()
    wdown = dt("wdown", [C, DFF], F32, kind="ExternalInput").ap()
    ctab_in = dt("ctab", [128, L], BF16, kind="ExternalInput").ap()
    stab_in = dt("stab", [128, L], BF16, kind="ExternalInput").ap()
    swp_in = dt("swp", [128, 128], BF16, kind="ExternalInput").ap()
    ones_in = dt("ones128", [128, 128], BF16, kind="ExternalInput").ap()
    blk_in = dt("blkones", [128, 128], BF16, kind="ExternalInput").ap()
    onesf_in = dt("onesf", [1, 128], F32, kind="ExternalInput").ap()
    ident_in = dt("identf", [128, 128], F32, kind="ExternalInput").ap()
    ones32_in = dt("ones32", [128, 128], F32, kind="ExternalInput").ap()
    cv_in = dt("cvals", [128, 8], F32, kind="ExternalInput").ap()
    y = dt("y", [L, C], F32, kind="ExternalOutput").ap()
    _sk = "ExternalOutput" if os.environ.get("KDBGOUT") else "Internal"
    sx = dt("sx", [L, C], BF16, kind=_sk).ap()
    skT = dt("skT", [C, L], BF16, kind=_sk).ap()
    sQ = dt("sQ", [C, L], BF16, kind=_sk).ap()
    sV = dt("sV", [NP * L, 130], BF16, kind=_sk).ap()
    sx2 = dt("sx2", [C, L], BF16, kind=_sk).ap()
    sqkv = dt("sqkv", [C, 3 * C], BF16, kind="Internal").ap()
    sprj = dt("sprj", [C, C], BF16, kind="Internal").ap()
    s0up = dt("s0up", [DFF, C], BF16, kind=_sk).ap()
    s0dn = dt("s0dn", [C, DFF], BF16, kind=_sk).ap()
    s1up = dt("s1up", [NFC * 128, C], BF16, kind=_sk).ap()
    s1dn = dt("s1dn", [NDC * 128, DFF], BF16, kind=_sk).ap()

    ops = {e: [] for e in ENGS}

    def em(eng, fn, waits=(), incs=()):
        for s, k in incs:
            s.n += k
        ops[eng].append((tuple((s, v) for (s, v) in waits if v > 0), fn, tuple(incs)))

    with contextlib.ExitStack() as ctx:
        E = ctx.enter_context
        big8 = E(nc.sbuf_tensor("big8", [128, 32768], BF16))
        kbuf = E(nc.sbuf_tensor("kbuf", [128, 8192], BF16))
        qbuf = E(nc.sbuf_tensor("qbuf", [128, 8192], BF16))
        vbuf = E(nc.sbuf_tensor("vbuf", [128, 8320], BF16))
        wbig = E(nc.sbuf_tensor("wbig", [128, 8192], BF16))
        wof = E(nc.sbuf_tensor("wof", [128, 2048], BF16))
        xq = E(nc.sbuf_tensor("xq", [128, 4096], BF16))
        pT = E(nc.sbuf_tensor("pT", [128, 4096], BF16))
        prj = E(nc.sbuf_tensor("prj", [128, 4096], BF16))
        sq16 = E(nc.sbuf_tensor("sq16", [128, 1024], BF16))
        ab16 = E(nc.sbuf_tensor("ab16", [128, 2048], BF16))
        st32 = E(nc.sbuf_tensor("st32", [128, 4096], F32))
        sm32 = E(nc.sbuf_tensor("sm32", [128, 256], F32))
        rcp32 = E(nc.sbuf_tensor("rcp32", [128, 512], F32))
        ctabs = E(nc.sbuf_tensor("ctabs", [128, 4096], BF16))
        stabs = E(nc.sbuf_tensor("stabs", [128, 4096], BF16))
        swp = E(nc.sbuf_tensor("swpS", [128, 128], BF16))
        ones = E(nc.sbuf_tensor("onesS", [128, 128], BF16))
        blko = E(nc.sbuf_tensor("blkoS", [128, 128], BF16))
        onesf = E(nc.sbuf_tensor("onesfS", [1, 128], F32))
        idf = E(nc.sbuf_tensor("idfS", [128, 128], F32))
        ones32 = E(nc.sbuf_tensor("ones32S", [128, 128], F32))
        cv = E(nc.sbuf_tensor("cvS", [128, 8], F32))
        arena = E(nc.psum_tensor("arena", [128, 2048], F32))
        pva = E(nc.psum_tensor("pva", [128, 512], F32))
        pvb = E(nc.psum_tensor("pvb", [128, 512], F32))
        pm1 = E(nc.psum_tensor("pm1", [128, 512], F32))
        pm2 = E(nc.psum_tensor("pm2", [128, 512], F32))
        dL = Sem(E(nc.semaphore("dL")))
        dW = Sem(E(nc.semaphore("dW")))
        dSs = Sem(E(nc.semaphore("dSs")))
        dG = Sem(E(nc.semaphore("dG")))
        dS = Sem(E(nc.semaphore("dS")))
        dO = Sem(E(nc.semaphore("dO")))
        tE = Sem(E(nc.semaphore("tE")))
        sE = Sem(E(nc.semaphore("sE")))
        vE = Sem(E(nc.semaphore("vE")))
        gE = Sem(E(nc.semaphore("gE")))
        blk = E(nc.Block())

        NONE = (dL, 0)

        def _finish():
            for sem_, eng_ in ((dL, "sync"), (dSs, "sync"), (dG, "gpsimd"),
                               (dW, "gpsimd"), (dS, "gpsimd"), (dO, "gpsimd")):
                v_ = sem_.n
                if v_ > 0:
                    ops[eng_].append(((), (lambda e, h=sem_.h, v=v_: e.wait_ge(h, v)), ()))

            def run(eng_name, proxy):
                for waits, fn, incs in ops[eng_name]:
                    for s_, v_ in waits:
                        proxy.wait_ge(s_.h, v_)
                    ins = fn(proxy)
                    for s_, k_ in incs:
                        ins.then_inc(s_.h, k_)

            @blk.sync
            def _(e):
                run("sync", e)

            @blk.tensor
            def _(e):
                run("tensor", e)

            @blk.scalar
            def _(e):
                run("scalar", e)

            @blk.vector
            def _(e):
                run("vector", e)

            @blk.gpsimd
            def _(e):
                run("gpsimd", e)
            return (nc, ops) if introspect else nc

        def dma(eng, out, in_, sem, waits=(), transpose=False):
            def f(e, out=out, in_=in_, transpose=transpose):
                if transpose:
                    return e.dma_start(out=out, in_=in_, transpose=True)
                return e.dma_start(out=out, in_=in_)
            em(eng, f, waits=waits, incs=[(sem, 16)])
            return sem.at()

        def mm(out, lhsT, rhs, start, stop, waits=(), inc=False):
            def f(e, out=out, lhsT=lhsT, rhs=rhs, start=start, stop=stop):
                return e.matmul(out, lhsT, rhs, start=start, stop=stop)
            em("tensor", f, waits=waits, incs=[(tE, 1)] if inc else [])
            return tE.at()

        ZERO = cv[:, 0:1]
        EPSC = cv[:, 1:2]

        def act(out, in_, func, bias=None, scale=1.0, accum_out=None, waits=()):
            if bias is None:
                b = 0.0 if func in (AF.Copy,) else ZERO

            else:
                b = bias

            def f(e, out=out, in_=in_, func=func, b=b, scale=scale, ao=accum_out):
                if ao is not None:
                    return e.activation(out, in_, func, bias=b, scale=scale,
                                        accum_out=ao)
                return e.activation(out, in_, func, bias=b, scale=scale)
            em("scalar", f, waits=waits, incs=[(sE, 1)])
            return sE.at()

        def vop(name, *args, waits=(), **kw):
            def f(e, name=name, args=args, kw=kw):
                return getattr(e, name)(*args, **kw)
            em("vector", f, waits=waits, incs=[(vE, 1)])
            return vE.at()

        def gop(name, *args, waits=(), **kw):
            def f(e, name=name, args=args, kw=kw):
                return getattr(e, name)(*args, **kw)
            em("gpsimd", f, waits=waits, incs=[(gE, 1)])
            return gE.at()

        def S(i):
            return st32[:, i * 512:(i + 1) * 512]
        s_mean, s_t1, s_var, s_std, s_a, s_b, s_u, s_v = (S(i) for i in range(8))
        xo32a = ctabs[:, :].bitcast(F32)
        xo32b = stabs[:, :].bitcast(F32)

        def xo32(dc):
            t = xo32a if dc < 4 else xo32b
            d = dc % 4
            return t[:, d * 512:(d + 1) * 512]
        outst = qbuf[:, 4096:8192].bitcast(F32)

        dma("sync", ctabs[:, :], ctab_in, dL)
        dma("sync", stabs[:, :], stab_in, dL)
        dma("sync", swp[:, :], swp_in, dL)
        dma("sync", ones[:, :], ones_in, dL)
        dma("sync", blko[:, :], blk_in, dL)
        dma("sync", onesf[:, :], onesf_in, dL)
        dma("sync", idf[:, :], ident_in, dL)
        dma("sync", ones32[:, :], ones32_in, dL)
        dma("sync", cv[:, :], cv_in, dL)
        consts_ld = dma("sync", sm32[:, 80:88], bprojT, dL)


        kbuf32 = kbuf[:, :].bitcast(F32)
        qbuf32 = qbuf[:, :].bitcast(F32)
        tern_state = {}

        def tern_pass1(src32, s0t, nrows, width, entry_waits, col0=192):
            nck = nrows // 128
            nslot = 8192 // (2 * width)
            buf_free = [NONE] * nslot
            rv = None
            for ck in range(nck):
                b = ck % nslot
                h = b % (nslot // 2) if nslot > 1 else 0
                src_b = kbuf32 if b < nslot // 2 or nslot == 1 else qbuf32
                buf = src_b[:, h * width:(h + 1) * width]
                w_ = [buf_free[b]] + (list(entry_waits) if ck == 0 else [])
                ld = dma("gpsimd", buf, src32[ck * 128:(ck + 1) * 128, :], dW,
                         waits=w_)
                rv = act(vbuf[:, 0:width], buf, AF.Abs,
                         accum_out=sm32[:, col0 + ck:col0 + ck + 1], waits=[ld])
                buf_free[b] = rv
                yield
            tern_state[id(s0t)] = dict(p1=rv, bufs=[rv, rv], col0=col0)

        def tern_alpha(s0t, nrows, width, pm1_waits=()):
            nelem = float(nrows * width)
            nck = nrows // 128
            st_ = tern_state[id(s0t)]
            c0 = st_["col0"]
            m_ = mm(pm1[:, 0:nck], ones32[:, :], sm32[:, c0:c0 + nck], True, True,
                    waits=[st_["p1"]] + list(pm1_waits), inc=True)
            a1_ = act(sm32[:, 224:224 + nck], pm1[:, 0:nck], AF.Copy,
                      accum_out=sm32[:, 98:99], waits=[m_])
            t1_ = act(sm32[:, 99:100], sm32[:, 98:99], AF.Copy, scale=0.5 / nelem,
                      waits=[a1_])
            t2_ = act(sm32[:, 100:101], sm32[:, 98:99], AF.Copy, scale=-0.5 / nelem)
            st_["tau"] = t2_

        def tern_pass2(src32, s0t, nrows, width):
            nck = nrows // 128
            st_ = tern_state[id(s0t)]
            tau, ntau = sm32[:, 99:100], sm32[:, 100:101]
            buf_free = [sE.at(), sE.at()]
            wt_wr = NONE
            for ck in range(nck):
                b = ck % 2
                buf = (kbuf32 if b == 0 else qbuf32)[:, 0:width]
                ld = dma("gpsimd", buf, src32[ck * 128:(ck + 1) * 128, :], dW,
                         waits=[buf_free[b]] + ([st_["tau"]] if ck == 0 else []))
                s1 = act(vbuf[:, 0:width], buf, AF.Sign, bias=ntau, waits=[ld])
                s2 = act(vbuf[:, 4160:4160 + width], buf, AF.Sign, bias=tau)
                buf_free[b] = s2
                wt = pT[:, 0:width]
                v1 = vop("tensor_tensor", wt, vbuf[:, 0:width],
                         vbuf[:, 4160:4160 + width], OP.add, waits=[s2, wt_wr])
                v2 = vop("tensor_scalar", wt, wt, 0.5, None, OP.mult, waits=[v1])
                wt_wr = dma("gpsimd", s0t[ck * 128:(ck + 1) * 128, :], wt, dS,
                            waits=[v2])
            st_["wt_done"] = dS.at()
            st_["v_done"] = vE.at()
            st_["g_done"] = gE.at()

        def tern_tail(s0t, dst, nrows, width, nstrips, kind, prj_free):
            st_ = tern_state[id(s0t)]
            prev_wr = [NONE, NONE]
            for st in range(nstrips):
                b = (st % 2) if nrows <= 2048 else 0
                half = prj[:, b * nrows:(b + 1) * nrows]
                w_ = [st_["wt_done"], prev_wr[b]]
                if st < 2:
                    w_ = w_ + list(prj_free)
                ldt = dma("sync", half, s0t[:, st * 128:(st + 1) * 128], dL,
                          waits=w_, transpose=True)
                if kind == "up":
                    out_ap = dst[:, st * 128:(st + 1) * 128].rearrange(
                        "(fc p) j -> p fc j", p=128)
                    in_ap = half.rearrange("p (fc j) -> p fc j", j=128)
                else:
                    out_ap = dst[:, st * 128:(st + 1) * 128].rearrange(
                        "(dc p) m -> p dc m", p=128)
                    in_ap = half.rearrange("p (dc m) -> p dc m", m=128)
                prev_wr[b] = dma("sync", out_ap, in_ap, dSs, waits=[ldt])

        PH = os.environ.get("KPHASES", "SABC")
        x_casts = []
        for tc_ in range(NTC):
            x_casts.append(dma("gpsimd", sx[tc_ * 512:(tc_ + 1) * 512, :],
                               x[tc_ * 512:(tc_ + 1) * 512, :], dG))
        qkv_cast = dma("gpsimd", sqkv, wqkv, dG)
        prj_cast = dma("gpsimd", sprj, wproj, dG)
        tg_up = tern_pass1(wup, s0up, DFF, C, [prj_cast])
        tg_dn = None

        if "A" not in PH:
            return _finish()
        xq_free = NONE
        big8_wr_wait = NONE
        xln_marks = []
        pva_free = (sE, 0)
        pvb_free = (vE, 0)
        sq_free_t = (tE, 0)
        for tc in range(NTC):
            xl = []
            for dc in range(NDC):
                w_ = [x_casts[tc], consts_ld, xq_free] if dc == 0 else []
                xl.append(dma("sync", xq[:, dc * 512:(dc + 1) * 512],
                              sx[tc * 512:(tc + 1) * 512, dc * 128:(dc + 1) * 128],
                              dL, waits=w_, transpose=True))
            smm = (tE, 0)
            for dc in range(NDC):
                sq = vop("tensor_tensor", sq16[:, (dc % 2) * 512:(dc % 2 + 1) * 512],
                         xq[:, dc * 512:(dc + 1) * 512], xq[:, dc * 512:(dc + 1) * 512],
                         OP.mult, waits=[xl[dc], sq_free_t])
                mm(pva[:, :], ones[:, :], xq[:, dc * 512:(dc + 1) * 512],
                   dc == 0, dc == NDC - 1,
                   waits=[xl[dc]] + ([pva_free] if dc == 0 else []))
                smm = mm(pvb[:, :], ones[:, :], sq16[:, (dc % 2) * 512:(dc % 2 + 1) * 512],
                         dc == 0, dc == NDC - 1,
                         waits=[sq] + ([pvb_free] if dc == 0 else []), inc=True)
                sq_free_t = smm
            e1 = act(s_mean, pva[:, :], AF.Copy, scale=1.0 / C, waits=[smm])
            v1 = vop("tensor_tensor", s_t1, s_mean, s_mean, OP.mult, waits=[e1])
            v2 = vop("scalar_tensor_tensor", s_var, pvb[:, :], 1.0 / C, s_t1,
                     OP.mult, OP.subtract, waits=[v1])
            pva_free, pvb_free = e1, v2
            e2 = act(s_std, s_var, AF.Sqrt, bias=EPSC, waits=[v2])
            v3 = vop("reciprocal", s_a, s_std, waits=[e2])
            v4 = vop("scalar_tensor_tensor", s_b, s_mean, -1.0, s_a, OP.mult, OP.mult,
                     waits=[v3])
            vop("tensor_copy", ab16[:, 0:512], s_a, waits=[v3])
            vb = vop("tensor_copy", ab16[:, 512:1024], s_b, waits=[v4])
            last = None
            for dc in range(NDC):
                w_ = [vb, sq_free_t] if dc == 0 else []
                if dc == 0:
                    w_.append(big8_wr_wait)
                vop("tensor_tensor", sq16[:, (dc % 2) * 512:(dc % 2 + 1) * 512],
                    xq[:, dc * 512:(dc + 1) * 512], ab16[:, 0:512], OP.mult, waits=w_)
                last = vop("tensor_tensor",
                           big8[:, dc * 4096 + tc * 512: dc * 4096 + (tc + 1) * 512 - (512 - 512)][:, 0:512]
                           if False else
                           big8[:, dc * 4096 + tc * 512: dc * 4096 + tc * 512 + 512],
                           sq16[:, (dc % 2) * 512:(dc % 2 + 1) * 512],
                           ab16[:, 512:1024], OP.add)
            sq_free_t = (tE, 0)
            xq_free = last
            xln_marks.append(last)
            for _ in range(4):
                next(tg_up, None)


        while next(tg_up, "d") != "d":
            pass
        tg_dn = tern_pass1(wdown, s0dn, C, DFF,
                           [tern_state[id(s0up)]["p1"]], col0=168)

        sl_free = [(sE, 0)] * 4
        wof_free = [(tE, 0), (tE, 0)]
        kraw_free = [[(vE, 0), (tE, 0)], [(vE, 0), (tE, 0)]]
        pm1_free = (vE, 0)
        pm2_free = (vE, 0)
        rop_free = (dS, 0)
        sq0_free = [(tE, 0), (tE, 0)]
        wls = {}
        items = [(of, tc) for of in range(16) for tc in range(NTC)]
        mmk_l, kb_l = {}, {}

        def a1_head(i):
            of, tc = items[i]
            b = of % 2
            if tc == 0:
                wls[of] = dma("sync",
                              wof[:, b * 1024:(b + 1) * 1024].rearrange("p (dc j) -> p dc j", j=128),
                              sqkv[:, of * 128:(of + 1) * 128].rearrange("(dc p) j -> p dc j", p=128),
                              dL, waits=[wof_free[b], qkv_cast])
            sl = i % 4
            aslice = arena[:, sl * 512:(sl + 1) * 512]
            for dc in range(NDC):
                w_ = [wls[of], sl_free[sl], xln_marks[tc]] if dc == 0 else []
                mmk = mm(aslice, wof[:, b * 1024 + dc * 128: b * 1024 + (dc + 1) * 128],
                         big8[:, dc * 4096 + tc * 512: dc * 4096 + tc * 512 + 512],
                         dc == 0, dc == NDC - 1, waits=w_, inc=(dc == NDC - 1))
            if tc == NTC - 1:
                wof_free[b] = mmk
            mmk_l[i] = (mmk, aslice)

        def a1_tail(i):
            nonlocal pm1_free, pm2_free, rop_free
            of, tc = items[i]
            mmk, aslice = mmk_l[i]
            kb_ = i % 2
            kraw = prj[:, kb_ * 512:(kb_ + 1) * 512]
            ev = act(kraw, aslice, AF.Copy, waits=[mmk] + kraw_free[kb_])
            sl_free[i % 4] = ev
            sw = mm(pm1[:, :], swp[:, :], kraw, True, True,
                    waits=[ev, pm1_free], inc=True)
            sq = vop("tensor_tensor", sq16[:, 0:512], kraw, kraw, OP.mult,
                     waits=[ev, sq0_free[0]])
            nm = mm(pm2[:, :], ones[:, :], sq16[:, 0:512], True, True,
                    waits=[sq, pm2_free], inc=True)
            sq0_free[0] = nm
            r1 = vop("tensor_tensor", s_u, kraw, ctabs[:, tc * 512:(tc + 1) * 512],
                     OP.mult, waits=[ev])
            r2 = vop("tensor_tensor", s_v, pm1[:, :], stabs[:, tc * 512:(tc + 1) * 512],
                     OP.mult, waits=[sw])
            pm1_free = r2
            rop = sq16[:, 512:1024]
            r3 = vop("tensor_tensor", rop, s_u, s_v, OP.add, waits=[rop_free])
            kraw_free[kb_] = [r3, sw]
            if of < 8:
                rm = vop("tensor_reduce", sm32[:, of * 8 + tc: of * 8 + tc + 1],
                         pm2[:, :], AX.X, OP.max, waits=[nm])
            else:
                if tc == 0:
                    rm = vop("tensor_reduce", sm32[:, 64 + of - 8: 65 + of - 8],
                             pm2[:, :], AX.X, OP.max, waits=[nm])
                else:
                    rv_ = vop("tensor_reduce", sm32[:, 101:102], pm2[:, :],
                              AX.X, OP.max, waits=[nm])
                    rm = vop("tensor_tensor", sm32[:, 64 + of - 8: 65 + of - 8],
                             sm32[:, 64 + of - 8: 65 + of - 8],
                             sm32[:, 101:102], OP.max, waits=[rv_])
            pm2_free = rm
            dstT = sQ if of < 8 else skT
            row0 = (of % 8) * 128
            rop_free = dma("sync", dstT[row0:row0 + 128, tc * 512:(tc + 1) * 512],
                           rop, dSs, waits=[r3])

        for i in range(len(items)):
            a1_head(i)
            if i >= 1:
                a1_tail(i - 1)
            if i % 8 == 0:
                next(tg_dn, None)
        a1_tail(len(items) - 1)
        while next(tg_dn, "d") != "d":
            pass

        a1_writes = dSs.at()
        a1_done_s = sE.at()
        a1_done_v = vE.at()

        bm = None
        for p in range(NP):
            bm = vop("tensor_scalar", sm32[:, 104 + p * 8: 104 + (p + 1) * 8],
                     sm32[:, p * 8: (p + 1) * 8], sm32[:, 64 + p: 65 + p], None,
                     OP.mult, waits=[a1_done_v] if p == 0 else [])
        bs = act(sm32[:, 104:168], sm32[:, 104:168], AF.Sqrt,
                 scale=(SCALE * BMARG) ** 2, waits=[bm])
        bias_done = vop("tensor_scalar", sm32[:, 104:168], sm32[:, 104:168],
                        -1.0, None, OP.mult, waits=[bs])

        tern_alpha(s0up, DFF, C, [pm1_free])
        tern_pass2(wup, s0up, DFF, C)
        tern_alpha(s0dn, C, DFF, [tern_state[id(s0up)]["tau"]])
        tern_pass2(wdown, s0dn, C, DFF)
        pm1_free = tern_state[id(s0dn)]["tau"]
        prj_free_t3 = [kraw_free[0][0], kraw_free[0][1],
                       kraw_free[1][0], kraw_free[1][1]]
        tern_tail(s0up, s1up, DFF, C, NDC, "up", prj_free_t3)

        wv_ld = None
        for dc in range(NDC):
            wv_ld = dma("sync", wbig[:, dc * 1024:(dc + 1) * 1024],
                        sqkv[dc * 128:(dc + 1) * 128, 2048:3072], dL,
                        waits=[qkv_cast] if dc == 0 else [])
        vst_free = [xq_free, xq_free]
        pmv_free = [pm1_free, pm2_free]
        ones_done = [False, False]
        for ti in range(L // 128):
            tc, tt = divmod(ti, 4)
            vb_ = ti % 2
            mmv = None
            for half in range(2):
                pmv = pm1 if half == 0 else pm2
                for dc in range(NDC):
                    w_ = [wv_ld, pmv_free[half], xln_marks[tc], a1_done_s] if dc == 0 else []
                    mmv = mm(pmv[:, :],
                             big8[:, dc * 4096 + tc * 512 + tt * 128: dc * 4096 + tc * 512 + tt * 128 + 128],
                             wbig[:, dc * 1024 + half * 512: dc * 1024 + (half + 1) * 512],
                             dc == 0, dc == NDC - 1, waits=w_, inc=(dc == NDC - 1))
            vstage = xq[:, vb_ * 1040:(vb_ + 1) * 1040]
            if not ones_done[vb_]:
                gop("memset", vstage.rearrange("p (h s) -> p h s", s=65)[:, :, 64:65],
                    1.0, waits=[vst_free[vb_]])
                ones_done[vb_] = True
            c1 = vop("tensor_copy",
                     vstage[:, 0:520].rearrange("p (h s) -> p h s", s=65)[:, :, 0:64],
                     pm1[:, :].rearrange("p (h s) -> p h s", s=64),
                     waits=[mmv, vst_free[vb_], a1_done_s])
            c2 = vop("tensor_copy",
                     vstage[:, 520:1040].rearrange("p (h s) -> p h s", s=65)[:, :, 0:64],
                     pm2[:, :].rearrange("p (h s) -> p h s", s=64))
            pmv_free = [c1, c2]
            last_w = None
            for p in range(NP):
                last_w = dma("gpsimd", sV[p * L + ti * 128: p * L + (ti + 1) * 128, :],
                             vstage[:, p * 130:(p + 1) * 130], dS,
                             waits=[c2] if p == 0 else [])
            vst_free[vb_] = last_w
        a2_writes = dS.at()
        a2_done_t = tE.at()
        a2_done_v = vE.at()

        if "B" not in PH:
            return _finish()
        wp_ld = None
        for dc in range(NDC):
            wp_ld = dma("sync", wbig[:, dc * 1024:(dc + 1) * 1024],
                        sprj[dc * 128:(dc + 1) * 128, :], dL,
                        waits=[a2_done_t, prj_cast] if dc == 0 else [])

        tern_up_state = tern_state[id(s0up)]
        tern_dn_state = tern_state[id(s0dn)]
        tern_free = [tern_dn_state["g_done"], tern_dn_state["wt_done"]]

        def pair_loads(p, buf, waits_extra):
            lk = dma("sync", kbuf[:, buf * 4096:(buf + 1) * 4096],
                     skT[p * 128:(p + 1) * 128, :], dL,
                     waits=[a1_writes] + list(waits_extra) +
                           (tern_free if p <= 1 else []))
            dma("sync", qbuf[:, buf * 4096:(buf + 1) * 4096],
                sQ[p * 128:(p + 1) * 128, :], dL)
            lv = None
            for part in range(4):
                lv = dma("sync",
                         vbuf[:, buf * 4160 + part * 1040: buf * 4160 + (part + 1) * 1040]
                         .rearrange("p (kt c) -> p kt c", c=130),
                         sV[p * L + part * 1024: p * L + (part + 1) * 1024, :]
                         .rearrange("(kt r) c -> r kt c", r=128),
                         dL, waits=[a2_writes] if part == 0 else [])
            return lv

        buf_free = [[], []]
        loads_done = [None, None]
        loads_done[0] = pair_loads(0, 0, [])
        arena_free = [a1_done_s, a1_done_s]
        pT_free = [tern_dn_state["wt_done"]] * 4 + [a2_writes] * 4
        s_den = [s_u, s_t1]
        den_free = [a1_done_v, a1_done_v]
        pvw = [[a2_done_v, a1_done_s], [a2_done_v, a1_done_s]]
        pmB_free = [a2_done_v, a2_done_v]
        first_exp = True
        tern_tail_done = None

        def pT_sl(ps):
            return (pT[:, ps * 1024:(ps + 1) * 1024] if ps < 4
                    else xq[:, (ps - 4) * 1024:(ps - 3) * 1024])

        for p in range(NP):
            buf = p % 2
            if p + 1 < NP:
                loads_done[(p + 1) % 2] = pair_loads(p + 1, (p + 1) % 2,
                                                     buf_free[(p + 1) % 2])
            if p == 1:
                tern_tail(s0dn, s1dn, C, DFF, NFC, "dn", [])
                tern_tail_done = dSs.at()
            ld = loads_done[buf]
            kp = kbuf[:, buf * 4096:(buf + 1) * 4096]
            qp = qbuf[:, buf * 4096:(buf + 1) * 4096]
            vp = vbuf[:, buf * 4160:(buf + 1) * 4160]

            exps = {}
            next_pv = 0

            def emit_norm(qc_, p=p):
                pvd = tE.at()
                for hh in range(2):
                    pv = pva if hh == 0 else pvb
                    pmb = pm1 if hh == 0 else pm2
                    cp = vop("tensor_copy", s_den[hh][0:1, :], pv[64:65, :],
                             waits=[pvd, den_free[hh]])
                    bc = mm(pmb[:, :], onesf[:, :], s_den[hh][0:1, :],
                            True, True, waits=[cp, pmB_free[hh]], inc=True)
                    den_free[hh] = bc
                    rc = vop("reciprocal", rcp32[0:64, :], pmb[0:64, :],
                             waits=[bc])
                    pmB_free[hh] = rc
                    nv = vop("tensor_tensor",
                             big8[hh * 64:(hh + 1) * 64,
                                  p * 4096 + qc_ * 512: p * 4096 + qc_ * 512 + 512],
                             pv[0:64, :], rcp32[0:64, :], OP.mult, waits=[rc])
                    pvw[hh] = [cp, nv]

            def emit_pv_flat(f, vp=vp):
                qc_, g_ = divmod(f, 32)
                ps = f % 8
                for hh in range(2):
                    pv = pva if hh == 0 else pvb
                    w_ = [exps[f]] if hh == 0 else []
                    if g_ == 0:
                        w_ = w_ + pvw[hh]
                    mm(pv[0:65, :],
                       vp[:, g_ * 130 + hh * 65: g_ * 130 + (hh + 1) * 65],
                       pT_sl(ps)[:, hh * 512:(hh + 1) * 512],
                       g_ == 0, g_ == 31, waits=w_, inc=(hh == 1))
                pT_free[ps] = tE.at()
                del exps[f]
                if g_ == 31:
                    emit_norm(qc_)

            for gg in range(NTC * 32):
                qc, g = divmod(gg, 32)
                bcol = sm32[:, 104 + p * 8 + qc: 105 + p * 8 + qc]
                ab = gg % 2
                asl = arena[:, ab * 1024:(ab + 1) * 1024]
                for hh in range(2):
                    w_ = [arena_free[ab], ld] if hh == 0 else []
                    mm(asl[:, hh * 512:(hh + 1) * 512],
                       kp[hh * 64:(hh + 1) * 64, g * 128:(g + 1) * 128],
                       qp[hh * 64:(hh + 1) * 64, qc * 512:(qc + 1) * 512],
                       True, True, waits=w_, inc=(hh == 1))
                sc_done = tE.at()
                cnt = 0
                while next_pv <= gg - 4 and (cnt < 1 or next_pv <= gg - 7):
                    emit_pv_flat(next_pv)
                    next_pv += 1
                    cnt += 1
                ps = gg % 8
                w_ = [sc_done, pT_free[ps]]
                if first_exp:
                    w_.append(bias_done)
                    first_exp = False
                ex = act(pT_sl(ps), asl, AF.Exp, bias=bcol, scale=SCALE, waits=w_)
                arena_free[ab] = ex
                exps[gg] = ex
            while next_pv < NTC * 32:
                emit_pv_flat(next_pv)
                next_pv += 1
            buf_free[buf] = [tE.at(), sE.at()]
        b1_done_v = vE.at()
        b1_done_t = tE.at()

        xq_freeB_l = [b1_done_t]
        prj_freeB = tern_tail_done
        arena_freeB = [arena_free[0], arena_free[0],
                       arena_free[1], arena_free[1]]
        pva_freeB = pvw[0][1]
        pvb_freeB = pvw[1][1]
        sqB_free = [a2_writes, a2_writes]
        sqB_free_t = b1_done_t
        x2_writes = None
        for qc in range(NTC):
            xl = []
            for dc in range(NDC):
                w_ = list(xq_freeB_l) if dc == 0 else []
                xl.append(dma("sync", xq[:, dc * 512:(dc + 1) * 512],
                              sx[qc * 512:(qc + 1) * 512, dc * 128:(dc + 1) * 128],
                              dL, waits=w_, transpose=True))
            smm = None
            for oc in range(NDC):
                sl = oc % 4
                aslice = arena[:, sl * 512:(sl + 1) * 512]
                for pc in range(NDC):
                    w_ = [arena_freeB[sl], wp_ld, b1_done_v] if pc == 0 else []
                    mmp = mm(aslice, wbig[:, pc * 1024 + oc * 128: pc * 1024 + (oc + 1) * 128],
                             big8[:, pc * 4096 + qc * 512: pc * 4096 + qc * 512 + 512],
                             pc == 0, pc == NDC - 1, waits=w_, inc=(pc == NDC - 1))
                ev = vop("tensor_scalar", prj[:, oc * 512:(oc + 1) * 512], aslice,
                         sm32[:, 80 + oc:81 + oc], None, OP.add,
                         waits=[mmp, prj_freeB])
                arena_freeB[sl] = ev
                sq = vop("tensor_tensor", sq16[:, (oc % 2) * 512:(oc % 2 + 1) * 512],
                         prj[:, oc * 512:(oc + 1) * 512], prj[:, oc * 512:(oc + 1) * 512],
                         OP.mult, waits=[ev, sqB_free[oc % 2], sqB_free_t])
                mm(pva[:, :], ones[:, :], prj[:, oc * 512:(oc + 1) * 512],
                   oc == 0, oc == NDC - 1,
                   waits=[ev] + ([pva_freeB] if oc == 0 else []))
                smm = mm(pvb[:, :], ones[:, :], sq16[:, (oc % 2) * 512:(oc % 2 + 1) * 512],
                         oc == 0, oc == NDC - 1,
                         waits=[sq] + ([pvb_freeB] if oc == 0 else []), inc=True)
                sqB_free_t = smm
            e1 = act(s_mean, pva[:, :], AF.Copy, scale=1.0 / C, waits=[smm])
            v1 = vop("tensor_tensor", s_t1, s_mean, s_mean, OP.mult, waits=[e1])
            v2 = vop("scalar_tensor_tensor", s_var, pvb[:, :], 1.0 / C, s_t1,
                     OP.mult, OP.subtract, waits=[v1])
            pva_freeB, pvb_freeB = e1, v2
            e2 = act(s_std, s_var, AF.Sqrt, bias=EPSC, waits=[v2])
            v3 = vop("reciprocal", s_a, s_std, waits=[e2])
            v4 = vop("scalar_tensor_tensor", s_b, s_mean, -1.0, s_a, OP.mult, OP.mult,
                     waits=[v3])
            last_v = None
            for oc in range(NDC):
                ob = oc % 2
                w_ = [v4, sqB_free_t] if oc == 0 else []
                vop("tensor_tensor", s_u, prj[:, oc * 512:(oc + 1) * 512], s_a,
                    OP.mult, waits=w_)
                vop("tensor_tensor", s_v, s_u, s_b, OP.add)
                x2s = vop("tensor_tensor", sq16[:, ob * 512:(ob + 1) * 512],
                          s_v, xq[:, oc * 512:(oc + 1) * 512], OP.add,
                          waits=[xl[oc], sqB_free[ob]])
                x2_writes = dma("gpsimd", sx2[oc * 128:(oc + 1) * 128, qc * 512:(qc + 1) * 512],
                                sq16[:, ob * 512:(ob + 1) * 512], dS, waits=[x2s])
                sqB_free[ob] = x2_writes
                last_v = x2s
            xq_freeB_l = [last_v]
            prj_freeB = last_v
            sqB_free_t = (tE, 0)
        b2_writes = dS.at()
        b2_done_s = sE.at()
        b2_done_v = vE.at()
        b2_done_t = tE.at()

        if "C" not in PH:
            return _finish()
        dno = kbuf[:, 4096:8192]

        def x2cb(i):
            return qbuf[:, 0:4096] if i % 2 == 0 else vbuf[:, 0:4096]

        def xnb(i):
            return kbuf[:, 0:4096] if i % 2 == 0 else vbuf[:, 4160:8256]

        def hb(i, fc):
            return big8[:, (i % 2) * 16384 + fc * 512:(i % 2) * 16384 + (fc + 1) * 512]

        st = {
            "x2_free": [[b1_done_t, (vE, 0)], [b1_done_t, (vE, 0)]],
            "xn_free": [b1_done_t, b1_done_t],
            "h_free": [b2_done_t, b2_done_t],
            "dno_free": (vE, 0),
            "pm_free": [[b2_done_s, b2_done_v], [b2_done_s, b2_done_v]],
            "arena01_free": [b2_done_s, b2_done_s],
            "outst_free": [(dO, 0), (dO, 0)],
            "xo_free": a1_done_v,
            "up_lw_free": [b2_done_v] * 4,
            "up_lw_mark": [None] * 4,
            "dn_lw_free": [b2_done_t, b2_done_t],
            "pv_free": [[b2_done_s, b2_done_v], [b2_done_s, b2_done_v]],
        }
        ln3_mark = {}
        ln4_state = {}

        def gen_ln3(i):
            lx2 = dma("sync", x2cb(i).rearrange("p (dc t) -> p dc t", t=512),
                      sx2[:, i * 512:(i + 1) * 512].rearrange("(dc p) t -> p dc t", p=128),
                      dL, waits=[b2_writes] + st["x2_free"][i % 2])
            yield
            x2c = x2cb(i)
            smm = None
            for dc in range(NDC):
                sq = vop("tensor_tensor", prj[:, (dc % 2) * 512:(dc % 2 + 1) * 512],
                         x2c[:, dc * 512:(dc + 1) * 512], x2c[:, dc * 512:(dc + 1) * 512],
                         OP.mult, waits=[lx2])
                mm(arena[:, 0:512], ones[:, :], x2c[:, dc * 512:(dc + 1) * 512],
                   dc == 0, dc == NDC - 1,
                   waits=[lx2] + ([st["arena01_free"][0]] if dc == 0 else []))
                smm = mm(arena[:, 512:1024], ones[:, :],
                         prj[:, (dc % 2) * 512:(dc % 2 + 1) * 512],
                         dc == 0, dc == NDC - 1,
                         waits=[sq] + ([st["arena01_free"][1]] if dc == 0 else []),
                         inc=True)
                yield
            e1 = act(s_mean, arena[:, 0:512], AF.Copy, scale=1.0 / C, waits=[smm])
            v1 = vop("tensor_tensor", s_t1, s_mean, s_mean, OP.mult, waits=[e1])
            v2 = vop("scalar_tensor_tensor", s_var, arena[:, 512:1024], 1.0 / C, s_t1,
                     OP.mult, OP.subtract, waits=[v1])
            st["arena01_free"] = [e1, v2]
            e2 = act(s_std, s_var, AF.Sqrt, bias=EPSC, waits=[v2])
            e3 = act(s_t1, s_var, AF.Sqrt, bias=ZERO, waits=[v2])
            v3 = vop("scalar_tensor_tensor", s_std, s_std, EPS_RMS, s_t1,
                     OP.mult, OP.add, waits=[e3, e2])
            v4 = vop("reciprocal", s_a, s_std, waits=[v3])
            v5 = vop("scalar_tensor_tensor", s_b, s_mean, -1.0, s_a, OP.mult, OP.mult,
                     waits=[v4])
            va = vop("tensor_copy", ab16[:, 0:512], s_a, waits=[v4])
            vb = vop("tensor_copy", ab16[:, 512:1024], s_b, waits=[v5])
            yield
            xn = xnb(i)
            xn_last = None
            for dc in range(NDC):
                w_ = [vb, st["xn_free"][i % 2]] if dc == 0 else []
                vop("tensor_tensor", prj[:, (dc % 2) * 512:(dc % 2 + 1) * 512],
                    x2c[:, dc * 512:(dc + 1) * 512], ab16[:, 0:512], OP.mult, waits=w_)
                xn_last = vop("tensor_tensor", xn[:, dc * 512:(dc + 1) * 512],
                              prj[:, (dc % 2) * 512:(dc % 2 + 1) * 512],
                              ab16[:, 512:1024], OP.add)
                yield
            ln3_mark[i] = xn_last

        def gen_ln4out(i):
            smm2, dn_last = ln4_state[i]
            e1 = act(s_mean, pm1[:, :], AF.Copy, scale=1.0 / C, waits=[smm2])
            v1 = vop("tensor_tensor", s_t1, s_mean, s_mean, OP.mult, waits=[e1])
            v2 = vop("scalar_tensor_tensor", s_var, pm2[:, :], 1.0 / C, s_t1,
                     OP.mult, OP.subtract, waits=[v1])
            st["pm_free"] = [[e1, v1], [v2, v2]]
            e2 = act(s_std, s_var, AF.Sqrt, bias=EPSC, waits=[v2])
            v3 = vop("reciprocal", s_a, s_std, waits=[e2])
            v4 = vop("scalar_tensor_tensor", s_b, s_mean, -1.0, s_a, OP.mult, OP.mult,
                     waits=[v3])
            yield
            xo_marks = []
            for dc in range(NDC):
                w_ = [v4] if dc > 0 else [v4, st["xo_free"]]
                vop("tensor_tensor", s_u, dno[:, dc * 512:(dc + 1) * 512], s_a,
                    OP.mult, waits=w_)
                vop("tensor_tensor", s_v, s_u, s_b, OP.add)
                xo = vop("tensor_tensor", xo32(dc), s_v,
                         x2cb(i)[:, dc * 512:(dc + 1) * 512], OP.add)
                xo_marks.append(xo)
                yield
            st["x2_free"][i % 2] = [xo_marks[-1]]
            st["dno_free"] = xo_marks[-1]
            prev_evs = None
            for tt in range(4):
                oh = tt % 2
                evs = []
                for dc in range(NDC):
                    pmt = pm1 if dc % 2 == 0 else pm2
                    if dc < 2:
                        if prev_evs is None:
                            w_ = [xo_marks[dc]] + st["pm_free"][dc]
                        else:
                            w_ = [xo_marks[dc], prev_evs[6 + dc]]
                    else:
                        w_ = [xo_marks[dc], evs[dc - 2]]

                    def tr(e, out=pmt[:, 0:128],
                           in_=xo32(dc)[:, tt * 128:tt * 128 + 128], ident=idf[:, :]):
                        return e.transpose(out, in_, ident)
                    em("tensor", tr, waits=w_, incs=[(tE, 1)])
                    trm = tE.at()
                    evv = vop("tensor_copy",
                              outst[:, oh * 1024 + dc * 128: oh * 1024 + (dc + 1) * 128],
                              pmt[:, 0:128], waits=[trm, st["outst_free"][oh]])
                    evs.append(evv)
                prev_evs = evs
                ow = dma("gpsimd", y[i * 512 + tt * 128: i * 512 + (tt + 1) * 128, :],
                         outst[:, oh * 1024:(oh + 1) * 1024], dO, waits=[evs[-1]])
                st["outst_free"][oh] = ow
                yield
            st["pm_free"] = [[(sE, 0), prev_evs[6]], [(sE, 0), prev_evs[7]]]
            st["xo_free"] = prev_evs[-1]

        def drain(g):
            if g is not None:
                while next(g, "d") != "d":
                    pass

        g3 = gen_ln3(0)
        drain(g3)
        g4 = None
        for qc in range(NTC):
            hsq = None
            pend_hsq = None

            def up_load(f):
                ub4 = f % 4
                st["up_lw_mark"][ub4] = dma(
                    "sync", xq[:, ub4 * 1024:(ub4 + 1) * 1024],
                    s1up[f * 128:(f + 1) * 128, :], dL,
                    waits=[tern_tail_done, st["up_lw_free"][ub4]])

            for f in range(3):
                up_load(f)
            for fc in range(NFC):
                ub4 = fc % 4
                lw = st["up_lw_mark"][ub4]
                sl = fc % 4
                aslice = arena[:, sl * 512:(sl + 1) * 512]
                for dc in range(NDC):
                    w_ = []
                    if dc == 0:
                        w_ = [lw, ln3_mark[qc]]
                        if fc < 2:
                            w_.append(st["arena01_free"][fc])
                        w_.append(st.get("gelu_free", {}).get(sl, (sE, 0)))
                    mmu = mm(aslice, xq[:, ub4 * 1024 + dc * 128: ub4 * 1024 + (dc + 1) * 128],
                             xnb(qc)[:, dc * 512:(dc + 1) * 512],
                             dc == 0, dc == NDC - 1, waits=w_, inc=(dc == NDC - 1))
                st["up_lw_free"][ub4] = mmu
                if fc + 3 < NFC:
                    up_load(fc + 3)
                if pend_hsq is not None:
                    hsq = pend_hsq()
                gl = act(hb(qc, fc), aslice, AF.Gelu,
                         waits=[mmu] + ([st["h_free"][qc % 2]] if fc == 0 else []))
                st.setdefault("gelu_free", {})[sl] = gl
                sqm = act(sq16[:, (fc % 2) * 512:(fc % 2 + 1) * 512], hb(qc, fc),
                          AF.Square,
                          waits=[st.get("sq16_free_t", (tE, 0))] if fc == 0 else [])

                def mk_hsq(fc=fc, sqm=sqm):
                    def go():
                        return mm(pva[:, :], ones[:, :],
                                  sq16[:, (fc % 2) * 512:(fc % 2 + 1) * 512],
                                  fc == 0, fc == NFC - 1,
                                  waits=[sqm] + (st["pv_free"][0] if fc == 0 else []),
                                  inc=(fc == NFC - 1))
                    return go
                pend_hsq = mk_hsq()
                if g4 is not None and fc >= 4 and fc % 2 == 0:
                    next(g4, None)
            hsq = pend_hsq()
            pend_hsq = None
            if g4 is not None:
                drain(g4)
            e4 = act(s_t1, pva[:, :], AF.Sqrt, bias=ZERO, scale=1.0 / DFF, waits=[hsq])
            v6 = vop("tensor_single_scalar", s_t1, s_t1, EPS_RMS, OP.add, waits=[e4])
            v7 = vop("reciprocal", s_u, s_t1, waits=[v6])
            gf = st.get("gelu_free", {})
            st["arena01_free"] = [gf.get(0, (sE, 0)), gf.get(1, (sE, 0))]
            g3 = gen_ln3(qc + 1) if qc + 1 < NTC else None
            smm2 = None
            dn_last = None
            sqd_free_t = hsq
            pend_pm = None
            for dc in range(NDC):
                db = dc % 2
                lw = dma("sync", wbig[:, db * 4096:(db + 1) * 4096],
                         s1dn[dc * 128:(dc + 1) * 128, :], dL,
                         waits=[tern_tail_done, st["dn_lw_free"][db]])
                pv = pva if db == 0 else pvb
                w0 = [lw, st["pv_free"][db][0], st["pv_free"][db][1]]
                if dc == 0:
                    w0.append(e4)
                for fc in range(NFC):
                    w_ = w0 if fc == 0 else []
                    mmd = mm(pv[:, :], wbig[:, db * 4096 + fc * 128: db * 4096 + (fc + 1) * 128],
                             big8[:, (qc % 2) * 16384 + fc * 512:(qc % 2) * 16384 + fc * 512 + 512],
                             fc == 0, fc == NFC - 1, waits=w_, inc=(fc == NFC - 1))
                st["dn_lw_free"][db] = mmd
                if pend_pm is not None:
                    sqd_free_t = pend_pm()
                ev = vop("tensor_tensor", dno[:, dc * 512:(dc + 1) * 512],
                         pv[:, :], s_u, OP.mult,
                         waits=[mmd, v7] + ([st["dno_free"]] if dc == 0 else []))
                st["pv_free"][db] = [ev, ev]
                dn_last = ev
                sq = vop("tensor_tensor", sq16[:, (dc % 2) * 512:(dc % 2 + 1) * 512],
                         dno[:, dc * 512:(dc + 1) * 512], dno[:, dc * 512:(dc + 1) * 512],
                         OP.mult, waits=[sqd_free_t])

                def mk_pm(dc=dc, ev=ev, sq=sq):
                    def go():
                        mm(pm1[:, :], ones[:, :], dno[:, dc * 512:(dc + 1) * 512],
                           dc == 0, dc == NDC - 1,
                           waits=[ev] + (st["pm_free"][0] if dc == 0 else []))
                        return mm(pm2[:, :], ones[:, :],
                                  sq16[:, (dc % 2) * 512:(dc % 2 + 1) * 512],
                                  dc == 0, dc == NDC - 1,
                                  waits=[sq] + (st["pm_free"][1] if dc == 0 else []),
                                  inc=True)
                    return go
                pend_pm = mk_pm()
                if g3 is not None:
                    next(g3, None)
                    next(g3, None)
            smm2 = pend_pm()
            st["h_free"][qc % 2] = smm2
            st["sq16_free_t"] = smm2
            st["xn_free"][qc % 2] = st["up_lw_free"][(NFC - 1) % 2]
            ln4_state[qc] = (smm2, dn_last)
            drain(g3)
            g4 = gen_ln4out(qc)
        drain(g4)

        return _finish()

        return _finish()


def make_consts():
    ROPE_BASE, ROPE_SCALE = 10000.0, 3.0
    j = np.arange(0, D, 2, dtype=np.float32)
    theta = (1.0 / ROPE_BASE ** (j / D)) * ROPE_SCALE
    t = np.arange(L, dtype=np.float32)
    ang = t[None, :] * theta[:, None]
    cosv, sinv = np.cos(ang), np.sin(ang)
    ctab = np.zeros((128, L), np.float32)
    stab = np.zeros((128, L), np.float32)
    for p in range(128):
        jj = (p % D) // 2
        ctab[p] = cosv[jj]
        stab[p] = sinv[jj] * (-1.0 if p % 2 == 0 else 1.0)
    swp = np.zeros((128, 128), np.float32)
    for k in range(128):
        swp[k, k ^ 1] = 1.0
    blkones = np.zeros((128, 128), np.float32)
    blkones[0:64, 0:64] = 1.0
    blkones[64:128, 64:128] = 1.0
    cvals = np.zeros((128, 8), np.float32)
    cvals[:, 1] = EPS_LN
    bf = ml_dtypes.bfloat16
    return dict(ctab=ctab.astype(bf), stab=stab.astype(bf), swp=swp.astype(bf),
                ones128=np.ones((128, 128), bf), blkones=blkones.astype(bf),
                onesf=np.ones((1, 128), np.float32),
                identf=np.eye(128, dtype=np.float32),
                ones32=np.ones((128, 128), np.float32), cvals=cvals)


def kernel(L0_attn_x, L0_attn_w_qkv, L0_attn_w_proj, L0_attn_b_proj,
           L0_ffn_w_up, L0_ffn_w_down):
    xs = np.ascontiguousarray(np.asarray(L0_attn_x, dtype=np.float32))
    wqkv = np.ascontiguousarray(np.asarray(L0_attn_w_qkv, dtype=np.float32))
    wproj = np.ascontiguousarray(np.asarray(L0_attn_w_proj, dtype=np.float32))
    bproj = np.asarray(L0_attn_b_proj, dtype=np.float32)
    wup = np.ascontiguousarray(np.asarray(L0_ffn_w_up, dtype=np.float32))
    wdown = np.ascontiguousarray(np.asarray(L0_ffn_w_down, dtype=np.float32))
    bprojT = np.ascontiguousarray(bproj.reshape(NDC, 128).T)

    consts = make_consts()
    nc = build_nc()
    in_maps = []
    for i in range(8):
        m = dict(x=np.ascontiguousarray(xs[i]), wqkv=wqkv, wproj=wproj,
                 bprojT=bprojT, wup=wup, wdown=wdown)
        m.update(consts)
        in_maps.append(m)
    res = run_bass_kernel_spmd(nc, in_maps, core_ids=list(range(8)))
    out = np.stack([np.asarray(r["y"], dtype=np.float32) for r in res.results], axis=0)
    return out


# version 18
# speedup vs baseline: 1.0355x; 1.0355x over previous
import contextlib
import os
import numpy as np
import ml_dtypes
import concourse.bass as bass
import concourse.mybir as mybir
from concourse.bass_utils import run_bass_kernel_spmd

F32 = mybir.dt.float32
BF16 = mybir.dt.bfloat16
AF = mybir.ActivationFunctionType
OP = mybir.AluOpType
AX = mybir.AxisListType

L = 4096
C = 1024
H = 16
D = 64
DFF = 4096
NP = 8
NDC = 8
NFC = 32
TQ = 512
NTC = L // TQ
EPS_LN = 1e-5
EPS_RMS = 1e-8
SCALE = 0.125
BMARG = 1.02

ENGS = ("sync", "tensor", "scalar", "vector", "gpsimd")


class Sem:
    def __init__(self, h):
        self.h = h
        self.n = 0

    def at(self):
        return (self, self.n)


def build_nc(introspect=False):
    nc = bass.Bass("TRN2", target_bir_lowering=False, debug=False,
                   detect_race_conditions=not os.environ.get("KSIMNORD"))
    dt = nc.dram_tensor
    x = dt("x", [L, C], F32, kind="ExternalInput").ap()
    wqkv = dt("wqkv", [C, 3 * C], F32, kind="ExternalInput").ap()
    wproj = dt("wproj", [C, C], F32, kind="ExternalInput").ap()
    bprojT = dt("bprojT", [128, NDC], F32, kind="ExternalInput").ap()
    wup = dt("wup", [DFF, C], F32, kind="ExternalInput").ap()
    wdown = dt("wdown", [C, DFF], F32, kind="ExternalInput").ap()
    ctab_in = dt("ctab", [128, L], BF16, kind="ExternalInput").ap()
    stab_in = dt("stab", [128, L], BF16, kind="ExternalInput").ap()
    swp_in = dt("swp", [128, 128], BF16, kind="ExternalInput").ap()
    ones_in = dt("ones128", [128, 128], BF16, kind="ExternalInput").ap()
    blk_in = dt("blkones", [128, 128], BF16, kind="ExternalInput").ap()
    onesf_in = dt("onesf", [1, 128], F32, kind="ExternalInput").ap()
    ident_in = dt("identf", [128, 128], F32, kind="ExternalInput").ap()
    ones32_in = dt("ones32", [128, 128], F32, kind="ExternalInput").ap()
    cv_in = dt("cvals", [128, 8], F32, kind="ExternalInput").ap()
    y = dt("y", [L, C], F32, kind="ExternalOutput").ap()
    _sk = "ExternalOutput" if os.environ.get("KDBGOUT") else "Internal"
    sx = dt("sx", [L, C], BF16, kind=_sk).ap()
    skT = dt("skT", [C, L], BF16, kind=_sk).ap()
    sQ = dt("sQ", [C, L], BF16, kind=_sk).ap()
    sV = dt("sV", [NP * L, 130], BF16, kind=_sk).ap()
    sx2 = dt("sx2", [C, L], BF16, kind=_sk).ap()
    sqkv = dt("sqkv", [C, 3 * C], BF16, kind="Internal").ap()
    sprj = dt("sprj", [C, C], BF16, kind="Internal").ap()
    s0up = dt("s0up", [DFF, C], BF16, kind=_sk).ap()
    s0dn = dt("s0dn", [C, DFF], BF16, kind=_sk).ap()
    s1up = dt("s1up", [NFC * 128, C], BF16, kind=_sk).ap()
    s1dn = dt("s1dn", [NDC * 128, DFF], BF16, kind=_sk).ap()

    ops = {e: [] for e in ENGS}

    def em(eng, fn, waits=(), incs=()):
        for s, k in incs:
            s.n += k
        ops[eng].append((tuple((s, v) for (s, v) in waits if v > 0), fn, tuple(incs)))

    with contextlib.ExitStack() as ctx:
        E = ctx.enter_context
        big8 = E(nc.sbuf_tensor("big8", [128, 32768], BF16))
        kbuf = E(nc.sbuf_tensor("kbuf", [128, 8192], BF16))
        qbuf = E(nc.sbuf_tensor("qbuf", [128, 8192], BF16))
        vbuf = E(nc.sbuf_tensor("vbuf", [128, 8320], BF16))
        wbig = E(nc.sbuf_tensor("wbig", [128, 8192], BF16))
        wof = E(nc.sbuf_tensor("wof", [128, 2048], BF16))
        xq = E(nc.sbuf_tensor("xq", [128, 4096], BF16))
        pT = E(nc.sbuf_tensor("pT", [128, 4096], BF16))
        prj = E(nc.sbuf_tensor("prj", [128, 4096], BF16))
        sq16 = E(nc.sbuf_tensor("sq16", [128, 1024], BF16))
        ab16 = E(nc.sbuf_tensor("ab16", [128, 2048], BF16))
        st32 = E(nc.sbuf_tensor("st32", [128, 4096], F32))
        sm32 = E(nc.sbuf_tensor("sm32", [128, 256], F32))
        rcp32 = E(nc.sbuf_tensor("rcp32", [128, 512], F32))
        ctabs = E(nc.sbuf_tensor("ctabs", [128, 4096], BF16))
        stabs = E(nc.sbuf_tensor("stabs", [128, 4096], BF16))
        swp = E(nc.sbuf_tensor("swpS", [128, 128], BF16))
        ones = E(nc.sbuf_tensor("onesS", [128, 128], BF16))
        blko = E(nc.sbuf_tensor("blkoS", [128, 128], BF16))
        onesf = E(nc.sbuf_tensor("onesfS", [1, 128], F32))
        idf = E(nc.sbuf_tensor("idfS", [128, 128], F32))
        ones32 = E(nc.sbuf_tensor("ones32S", [128, 128], F32))
        cv = E(nc.sbuf_tensor("cvS", [128, 8], F32))
        arena = E(nc.psum_tensor("arena", [128, 2048], F32))
        pva = E(nc.psum_tensor("pva", [128, 512], F32))
        pvb = E(nc.psum_tensor("pvb", [128, 512], F32))
        pm1 = E(nc.psum_tensor("pm1", [128, 512], F32))
        pm2 = E(nc.psum_tensor("pm2", [128, 512], F32))
        dL = Sem(E(nc.semaphore("dL")))
        dW = Sem(E(nc.semaphore("dW")))
        dSs = Sem(E(nc.semaphore("dSs")))
        dG = Sem(E(nc.semaphore("dG")))
        dS = Sem(E(nc.semaphore("dS")))
        dO = Sem(E(nc.semaphore("dO")))
        tE = Sem(E(nc.semaphore("tE")))
        sE = Sem(E(nc.semaphore("sE")))
        vE = Sem(E(nc.semaphore("vE")))
        gE = Sem(E(nc.semaphore("gE")))
        blk = E(nc.Block())

        NONE = (dL, 0)

        def _finish():
            for sem_, eng_ in ((dL, "sync"), (dSs, "sync"), (dG, "gpsimd"),
                               (dW, "gpsimd"), (dS, "gpsimd"), (dO, "gpsimd")):
                v_ = sem_.n
                if v_ > 0:
                    ops[eng_].append(((), (lambda e, h=sem_.h, v=v_: e.wait_ge(h, v)), ()))

            def run(eng_name, proxy):
                for waits, fn, incs in ops[eng_name]:
                    for s_, v_ in waits:
                        proxy.wait_ge(s_.h, v_)
                    ins = fn(proxy)
                    for s_, k_ in incs:
                        ins.then_inc(s_.h, k_)

            @blk.sync
            def _(e):
                run("sync", e)

            @blk.tensor
            def _(e):
                run("tensor", e)

            @blk.scalar
            def _(e):
                run("scalar", e)

            @blk.vector
            def _(e):
                run("vector", e)

            @blk.gpsimd
            def _(e):
                run("gpsimd", e)
            return (nc, ops) if introspect else nc

        def dma(eng, out, in_, sem, waits=(), transpose=False):
            def f(e, out=out, in_=in_, transpose=transpose):
                if transpose:
                    return e.dma_start(out=out, in_=in_, transpose=True)
                return e.dma_start(out=out, in_=in_)
            em(eng, f, waits=waits, incs=[(sem, 16)])
            return sem.at()

        def mm(out, lhsT, rhs, start, stop, waits=(), inc=False):
            def f(e, out=out, lhsT=lhsT, rhs=rhs, start=start, stop=stop):
                return e.matmul(out, lhsT, rhs, start=start, stop=stop)
            em("tensor", f, waits=waits, incs=[(tE, 1)] if inc else [])
            return tE.at()

        ZERO = cv[:, 0:1]
        EPSC = cv[:, 1:2]

        def act(out, in_, func, bias=None, scale=1.0, accum_out=None, waits=()):
            if bias is None:
                b = 0.0 if func in (AF.Copy,) else ZERO

            else:
                b = bias

            def f(e, out=out, in_=in_, func=func, b=b, scale=scale, ao=accum_out):
                if ao is not None:
                    return e.activation(out, in_, func, bias=b, scale=scale,
                                        accum_out=ao)
                return e.activation(out, in_, func, bias=b, scale=scale)
            em("scalar", f, waits=waits, incs=[(sE, 1)])
            return sE.at()

        def vop(name, *args, waits=(), **kw):
            def f(e, name=name, args=args, kw=kw):
                return getattr(e, name)(*args, **kw)
            em("vector", f, waits=waits, incs=[(vE, 1)])
            return vE.at()

        def gop(name, *args, waits=(), **kw):
            def f(e, name=name, args=args, kw=kw):
                return getattr(e, name)(*args, **kw)
            em("gpsimd", f, waits=waits, incs=[(gE, 1)])
            return gE.at()

        def S(i):
            return st32[:, i * 512:(i + 1) * 512]
        s_mean, s_t1, s_var, s_std, s_a, s_b, s_u, s_v = (S(i) for i in range(8))
        xo32a = ctabs[:, :].bitcast(F32)
        xo32b = stabs[:, :].bitcast(F32)

        def xo32(dc):
            t = xo32a if dc < 4 else xo32b
            d = dc % 4
            return t[:, d * 512:(d + 1) * 512]
        outst = qbuf[:, 4096:8192].bitcast(F32)

        dma("sync", ctabs[:, :], ctab_in, dL)
        dma("sync", stabs[:, :], stab_in, dL)
        dma("sync", swp[:, :], swp_in, dL)
        dma("sync", ones[:, :], ones_in, dL)
        dma("sync", blko[:, :], blk_in, dL)
        dma("sync", onesf[:, :], onesf_in, dL)
        dma("sync", idf[:, :], ident_in, dL)
        dma("sync", ones32[:, :], ones32_in, dL)
        dma("sync", cv[:, :], cv_in, dL)
        consts_ld = dma("sync", sm32[:, 80:88], bprojT, dL)


        kbuf32 = kbuf[:, :].bitcast(F32)
        qbuf32 = qbuf[:, :].bitcast(F32)
        tern_state = {}

        def tern_pass1(src32, s0t, nrows, width, entry_waits, col0=192):
            nck = nrows // 128
            nslot = 8192 // (2 * width)
            buf_free = [NONE] * nslot
            rv = None
            for ck in range(nck):
                b = ck % nslot
                h = b % (nslot // 2) if nslot > 1 else 0
                src_b = kbuf32 if b < nslot // 2 or nslot == 1 else qbuf32
                buf = src_b[:, h * width:(h + 1) * width]
                w_ = [buf_free[b]] + (list(entry_waits) if ck == 0 else [])
                ld = dma("gpsimd", buf, src32[ck * 128:(ck + 1) * 128, :], dW,
                         waits=w_)
                rv = act(vbuf[:, 0:width], buf, AF.Abs,
                         accum_out=sm32[:, col0 + ck:col0 + ck + 1], waits=[ld])
                buf_free[b] = rv
                yield
            tern_state[id(s0t)] = dict(p1=rv, bufs=[rv, rv], col0=col0)

        def tern_alpha(s0t, nrows, width, pm1_waits=()):
            nelem = float(nrows * width)
            nck = nrows // 128
            st_ = tern_state[id(s0t)]
            c0 = st_["col0"]
            m_ = mm(pm1[:, 0:nck], ones32[:, :], sm32[:, c0:c0 + nck], True, True,
                    waits=[st_["p1"]] + list(pm1_waits), inc=True)
            a1_ = act(sm32[:, 224:224 + nck], pm1[:, 0:nck], AF.Copy,
                      accum_out=sm32[:, 98:99], waits=[m_])
            t1_ = act(sm32[:, 99:100], sm32[:, 98:99], AF.Copy, scale=0.5 / nelem,
                      waits=[a1_])
            t2_ = act(sm32[:, 100:101], sm32[:, 98:99], AF.Copy, scale=-0.5 / nelem)
            st_["tau"] = t2_

        def tern_pass2(src32, s0t, nrows, width):
            nck = nrows // 128
            st_ = tern_state[id(s0t)]
            tau, ntau = sm32[:, 99:100], sm32[:, 100:101]
            buf_free = [sE.at(), sE.at()]
            wt_wr = NONE
            for ck in range(nck):
                b = ck % 2
                buf = (kbuf32 if b == 0 else qbuf32)[:, 0:width]
                ld = dma("gpsimd", buf, src32[ck * 128:(ck + 1) * 128, :], dW,
                         waits=[buf_free[b]] + ([st_["tau"]] if ck == 0 else []))
                s1 = act(vbuf[:, 0:width], buf, AF.Sign, bias=ntau, waits=[ld])
                s2 = act(vbuf[:, 4160:4160 + width], buf, AF.Sign, bias=tau)
                buf_free[b] = s2
                wt = pT[:, 0:width]
                v1 = vop("tensor_tensor", wt, vbuf[:, 0:width],
                         vbuf[:, 4160:4160 + width], OP.add, waits=[s2, wt_wr])
                v2 = vop("tensor_scalar", wt, wt, 0.5, None, OP.mult, waits=[v1])
                wt_wr = dma("gpsimd", s0t[ck * 128:(ck + 1) * 128, :], wt, dS,
                            waits=[v2])
            st_["wt_done"] = dS.at()
            st_["v_done"] = vE.at()
            st_["g_done"] = gE.at()

        def tern_tail(s0t, dst, nrows, width, nstrips, kind, prj_free):
            st_ = tern_state[id(s0t)]
            prev_wr = [NONE, NONE]
            for st in range(nstrips):
                b = (st % 2) if nrows <= 2048 else 0
                half = prj[:, b * nrows:(b + 1) * nrows]
                w_ = [st_["wt_done"], prev_wr[b]]
                if st < 2:
                    w_ = w_ + list(prj_free)
                ldt = dma("sync", half, s0t[:, st * 128:(st + 1) * 128], dL,
                          waits=w_, transpose=True)
                if kind == "up":
                    out_ap = dst[:, st * 128:(st + 1) * 128].rearrange(
                        "(fc p) j -> p fc j", p=128)
                    in_ap = half.rearrange("p (fc j) -> p fc j", j=128)
                else:
                    out_ap = dst[:, st * 128:(st + 1) * 128].rearrange(
                        "(dc p) m -> p dc m", p=128)
                    in_ap = half.rearrange("p (dc m) -> p dc m", m=128)
                prev_wr[b] = dma("sync", out_ap, in_ap, dSs, waits=[ldt])

        PH = os.environ.get("KPHASES", "SABC")
        x_casts = []
        for tc_ in range(NTC):
            x_casts.append(dma("gpsimd", sx[tc_ * 512:(tc_ + 1) * 512, :],
                               x[tc_ * 512:(tc_ + 1) * 512, :], dG))
        qkv_cast = dma("gpsimd", sqkv, wqkv, dG)
        prj_cast = dma("gpsimd", sprj, wproj, dG)
        tg_up = tern_pass1(wup, s0up, DFF, C, [prj_cast])
        tg_dn = None

        if "A" not in PH:
            return _finish()
        xq_free = NONE
        big8_wr_wait = NONE
        xln_marks = []
        pva_free = (sE, 0)
        pvb_free = (vE, 0)
        sq_free_t = (tE, 0)
        for tc in range(NTC):
            xl = []
            for dc in range(NDC):
                w_ = [x_casts[tc], consts_ld, xq_free] if dc == 0 else []
                xl.append(dma("sync", xq[:, dc * 512:(dc + 1) * 512],
                              sx[tc * 512:(tc + 1) * 512, dc * 128:(dc + 1) * 128],
                              dL, waits=w_, transpose=True))
            smm = (tE, 0)
            for dc in range(NDC):
                sq = vop("tensor_tensor", sq16[:, (dc % 2) * 512:(dc % 2 + 1) * 512],
                         xq[:, dc * 512:(dc + 1) * 512], xq[:, dc * 512:(dc + 1) * 512],
                         OP.mult, waits=[xl[dc], sq_free_t])
                mm(pva[:, :], ones[:, :], xq[:, dc * 512:(dc + 1) * 512],
                   dc == 0, dc == NDC - 1,
                   waits=[xl[dc]] + ([pva_free] if dc == 0 else []))
                smm = mm(pvb[:, :], ones[:, :], sq16[:, (dc % 2) * 512:(dc % 2 + 1) * 512],
                         dc == 0, dc == NDC - 1,
                         waits=[sq] + ([pvb_free] if dc == 0 else []), inc=True)
                sq_free_t = smm
            e1 = act(s_mean, pva[:, :], AF.Copy, scale=1.0 / C, waits=[smm])
            v1 = vop("tensor_tensor", s_t1, s_mean, s_mean, OP.mult, waits=[e1])
            v2 = vop("scalar_tensor_tensor", s_var, pvb[:, :], 1.0 / C, s_t1,
                     OP.mult, OP.subtract, waits=[v1])
            pva_free, pvb_free = e1, v2
            e2 = act(s_std, s_var, AF.Sqrt, bias=EPSC, waits=[v2])
            v3 = vop("reciprocal", s_a, s_std, waits=[e2])
            v4 = vop("scalar_tensor_tensor", s_b, s_mean, -1.0, s_a, OP.mult, OP.mult,
                     waits=[v3])
            vop("tensor_copy", ab16[:, 0:512], s_a, waits=[v3])
            vb = vop("tensor_copy", ab16[:, 512:1024], s_b, waits=[v4])
            last = None
            for dc in range(NDC):
                w_ = [vb, sq_free_t] if dc == 0 else []
                if dc == 0:
                    w_.append(big8_wr_wait)
                vop("tensor_tensor", sq16[:, (dc % 2) * 512:(dc % 2 + 1) * 512],
                    xq[:, dc * 512:(dc + 1) * 512], ab16[:, 0:512], OP.mult, waits=w_)
                last = vop("tensor_tensor",
                           big8[:, dc * 4096 + tc * 512: dc * 4096 + (tc + 1) * 512 - (512 - 512)][:, 0:512]
                           if False else
                           big8[:, dc * 4096 + tc * 512: dc * 4096 + tc * 512 + 512],
                           sq16[:, (dc % 2) * 512:(dc % 2 + 1) * 512],
                           ab16[:, 512:1024], OP.add)
            sq_free_t = (tE, 0)
            xq_free = last
            xln_marks.append(last)
            for _ in range(4):
                next(tg_up, None)


        while next(tg_up, "d") != "d":
            pass
        tg_dn = tern_pass1(wdown, s0dn, C, DFF,
                           [tern_state[id(s0up)]["p1"]], col0=168)

        sl_free = [(sE, 0)] * 4
        wof_free = [(tE, 0), (tE, 0)]
        kraw_free = [[(vE, 0), (tE, 0)], [(vE, 0), (tE, 0)]]
        pm1_free = (vE, 0)
        pm2_free = (vE, 0)
        rop_free = (dS, 0)
        sq0_free = [(tE, 0), (tE, 0)]
        wls = {}
        items = [(of, tc) for of in range(16) for tc in range(NTC)]
        mmk_l, kb_l = {}, {}

        def a1_head(i):
            of, tc = items[i]
            b = of % 2
            if tc == 0:
                wls[of] = dma("sync",
                              wof[:, b * 1024:(b + 1) * 1024].rearrange("p (dc j) -> p dc j", j=128),
                              sqkv[:, of * 128:(of + 1) * 128].rearrange("(dc p) j -> p dc j", p=128),
                              dL, waits=[wof_free[b], qkv_cast])
            sl = i % 4
            aslice = arena[:, sl * 512:(sl + 1) * 512]
            for dc in range(NDC):
                w_ = [wls[of], sl_free[sl], xln_marks[tc]] if dc == 0 else []
                mmk = mm(aslice, wof[:, b * 1024 + dc * 128: b * 1024 + (dc + 1) * 128],
                         big8[:, dc * 4096 + tc * 512: dc * 4096 + tc * 512 + 512],
                         dc == 0, dc == NDC - 1, waits=w_, inc=(dc == NDC - 1))
            if tc == NTC - 1:
                wof_free[b] = mmk
            mmk_l[i] = (mmk, aslice)

        def a1_tail(i):
            nonlocal pm1_free, pm2_free, rop_free
            of, tc = items[i]
            mmk, aslice = mmk_l[i]
            kb_ = i % 2
            kraw = prj[:, kb_ * 512:(kb_ + 1) * 512]
            ev = act(kraw, aslice, AF.Copy, waits=[mmk] + kraw_free[kb_])
            sl_free[i % 4] = ev
            sw = mm(pm1[:, :], swp[:, :], kraw, True, True,
                    waits=[ev, pm1_free], inc=True)
            sq = act(sq16[:, 0:512], kraw, AF.Square,
                     waits=[ev, sq0_free[0]])
            nm = mm(pm2[:, :], ones[:, :], sq16[:, 0:512], True, True,
                    waits=[sq, pm2_free], inc=True)
            sq0_free[0] = nm
            r1 = vop("tensor_tensor", s_u, kraw, ctabs[:, tc * 512:(tc + 1) * 512],
                     OP.mult, waits=[ev])
            r2 = vop("tensor_tensor", s_v, pm1[:, :], stabs[:, tc * 512:(tc + 1) * 512],
                     OP.mult, waits=[sw])
            pm1_free = r2
            rop = sq16[:, 512:1024]
            r3 = vop("tensor_tensor", rop, s_u, s_v, OP.add, waits=[rop_free])
            kraw_free[kb_] = [r3, sw]
            if of < 8:
                rm = vop("tensor_reduce", sm32[:, of * 8 + tc: of * 8 + tc + 1],
                         pm2[:, :], AX.X, OP.max, waits=[nm])
            else:
                if tc == 0:
                    rm = vop("tensor_reduce", sm32[:, 64 + of - 8: 65 + of - 8],
                             pm2[:, :], AX.X, OP.max, waits=[nm])
                else:
                    rv_ = vop("tensor_reduce", sm32[:, 101:102], pm2[:, :],
                              AX.X, OP.max, waits=[nm])
                    rm = vop("tensor_tensor", sm32[:, 64 + of - 8: 65 + of - 8],
                             sm32[:, 64 + of - 8: 65 + of - 8],
                             sm32[:, 101:102], OP.max, waits=[rv_])
            pm2_free = rm
            dstT = sQ if of < 8 else skT
            row0 = (of % 8) * 128
            rop_free = dma("sync", dstT[row0:row0 + 128, tc * 512:(tc + 1) * 512],
                           rop, dSs, waits=[r3])

        for i in range(len(items)):
            a1_head(i)
            if i >= 1:
                a1_tail(i - 1)
            if i % 8 == 0:
                next(tg_dn, None)
        a1_tail(len(items) - 1)
        while next(tg_dn, "d") != "d":
            pass

        a1_writes = dSs.at()
        a1_done_s = sE.at()
        a1_done_v = vE.at()

        bm = None
        for p in range(NP):
            bm = vop("tensor_scalar", sm32[:, 104 + p * 8: 104 + (p + 1) * 8],
                     sm32[:, p * 8: (p + 1) * 8], sm32[:, 64 + p: 65 + p], None,
                     OP.mult, waits=[a1_done_v] if p == 0 else [])
        bs = act(sm32[:, 104:168], sm32[:, 104:168], AF.Sqrt,
                 scale=(SCALE * BMARG) ** 2, waits=[bm])
        bias_done = vop("tensor_scalar", sm32[:, 104:168], sm32[:, 104:168],
                        -1.0, None, OP.mult, waits=[bs])

        tern_alpha(s0up, DFF, C, [pm1_free])
        tern_pass2(wup, s0up, DFF, C)
        tern_alpha(s0dn, C, DFF, [tern_state[id(s0up)]["tau"]])
        tern_pass2(wdown, s0dn, C, DFF)
        pm1_free = tern_state[id(s0dn)]["tau"]
        prj_free_t3 = [kraw_free[0][0], kraw_free[0][1],
                       kraw_free[1][0], kraw_free[1][1]]
        tern_tail(s0up, s1up, DFF, C, NDC, "up", prj_free_t3)

        wv_ld = None
        for dc in range(NDC):
            wv_ld = dma("sync", wbig[:, dc * 1024:(dc + 1) * 1024],
                        sqkv[dc * 128:(dc + 1) * 128, 2048:3072], dL,
                        waits=[qkv_cast] if dc == 0 else [])
        vst_free = [xq_free, xq_free]
        pmv_free = [pm1_free, pm2_free]
        ones_done = [False, False]
        for ti in range(L // 128):
            tc, tt = divmod(ti, 4)
            vb_ = ti % 2
            mmv = None
            for half in range(2):
                pmv = pm1 if half == 0 else pm2
                for dc in range(NDC):
                    w_ = [wv_ld, pmv_free[half], xln_marks[tc], a1_done_s] if dc == 0 else []
                    mmv = mm(pmv[:, :],
                             big8[:, dc * 4096 + tc * 512 + tt * 128: dc * 4096 + tc * 512 + tt * 128 + 128],
                             wbig[:, dc * 1024 + half * 512: dc * 1024 + (half + 1) * 512],
                             dc == 0, dc == NDC - 1, waits=w_, inc=(dc == NDC - 1))
            vstage = xq[:, vb_ * 1040:(vb_ + 1) * 1040]
            if not ones_done[vb_]:
                gop("memset", vstage.rearrange("p (h s) -> p h s", s=65)[:, :, 64:65],
                    1.0, waits=[vst_free[vb_]])
                ones_done[vb_] = True
            c1 = vop("tensor_copy",
                     vstage[:, 0:520].rearrange("p (h s) -> p h s", s=65)[:, :, 0:64],
                     pm1[:, :].rearrange("p (h s) -> p h s", s=64),
                     waits=[mmv, vst_free[vb_], a1_done_s])
            c2 = vop("tensor_copy",
                     vstage[:, 520:1040].rearrange("p (h s) -> p h s", s=65)[:, :, 0:64],
                     pm2[:, :].rearrange("p (h s) -> p h s", s=64))
            pmv_free = [c1, c2]
            last_w = None
            for p in range(NP):
                last_w = dma("gpsimd", sV[p * L + ti * 128: p * L + (ti + 1) * 128, :],
                             vstage[:, p * 130:(p + 1) * 130], dS,
                             waits=[c2] if p == 0 else [])
            vst_free[vb_] = last_w
        a2_writes = dS.at()
        a2_done_t = tE.at()
        a2_done_v = vE.at()

        if "B" not in PH:
            return _finish()
        wp_ld = None
        for dc in range(NDC):
            wp_ld = dma("sync", wbig[:, dc * 1024:(dc + 1) * 1024],
                        sprj[dc * 128:(dc + 1) * 128, :], dL,
                        waits=[a2_done_t, prj_cast] if dc == 0 else [])

        tern_up_state = tern_state[id(s0up)]
        tern_dn_state = tern_state[id(s0dn)]
        tern_free = [tern_dn_state["g_done"], tern_dn_state["wt_done"]]

        def pair_loads(p, buf, waits_extra):
            lk = dma("sync", kbuf[:, buf * 4096:(buf + 1) * 4096],
                     skT[p * 128:(p + 1) * 128, :], dL,
                     waits=[a1_writes] + list(waits_extra) +
                           (tern_free if p <= 1 else []))
            dma("sync", qbuf[:, buf * 4096:(buf + 1) * 4096],
                sQ[p * 128:(p + 1) * 128, :], dL)
            lv = None
            for part in range(4):
                lv = dma("sync",
                         vbuf[:, buf * 4160 + part * 1040: buf * 4160 + (part + 1) * 1040]
                         .rearrange("p (kt c) -> p kt c", c=130),
                         sV[p * L + part * 1024: p * L + (part + 1) * 1024, :]
                         .rearrange("(kt r) c -> r kt c", r=128),
                         dL, waits=[a2_writes] if part == 0 else [])
            return lv

        buf_free = [[], []]
        loads_done = [None, None]
        loads_done[0] = pair_loads(0, 0, [])
        arena_free = [a1_done_s, a1_done_s]
        pT_free = [tern_dn_state["wt_done"]] * 4 + [a2_writes] * 4
        s_den = [s_u, s_t1]
        den_free = [a1_done_v, a1_done_v]
        pvw = [[a2_done_v, a1_done_s], [a2_done_v, a1_done_s]]
        pmB_free = [a2_done_v, a2_done_v]
        first_exp = True
        tern_tail_done = None

        def pT_sl(ps):
            return (pT[:, ps * 1024:(ps + 1) * 1024] if ps < 4
                    else xq[:, (ps - 4) * 1024:(ps - 3) * 1024])

        for p in range(NP):
            buf = p % 2
            if p + 1 < NP:
                loads_done[(p + 1) % 2] = pair_loads(p + 1, (p + 1) % 2,
                                                     buf_free[(p + 1) % 2])
            if p == 1:
                tern_tail(s0dn, s1dn, C, DFF, NFC, "dn", [])
                tern_tail_done = dSs.at()
            ld = loads_done[buf]
            kp = kbuf[:, buf * 4096:(buf + 1) * 4096]
            qp = qbuf[:, buf * 4096:(buf + 1) * 4096]
            vp = vbuf[:, buf * 4160:(buf + 1) * 4160]

            exps = {}
            stage = []
            next_pv = 0

            def emit_norm(qc_, p=p):
                pvd = tE.at()
                bcm = {}
                cpm = {}

                def mk_cb(hh):
                    def go():
                        pv = pva if hh == 0 else pvb
                        pmb = pm1 if hh == 0 else pm2
                        cp = vop("tensor_copy", s_den[hh][0:1, :], pv[64:65, :],
                                 waits=[pvd, den_free[hh]])
                        cpm[hh] = cp
                        bc = mm(pmb[:, :], onesf[:, :], s_den[hh][0:1, :],
                                True, True, waits=[cp, pmB_free[hh]], inc=True)
                        den_free[hh] = bc
                        bcm[hh] = bc
                    return go

                def mk_v(hh):
                    def go():
                        pv = pva if hh == 0 else pvb
                        pmb = pm1 if hh == 0 else pm2
                        rc = vop("reciprocal", rcp32[0:64, :], pmb[0:64, :],
                                 waits=[bcm[hh]])
                        pmB_free[hh] = rc
                        nv = vop("tensor_tensor",
                                 big8[hh * 64:(hh + 1) * 64,
                                      p * 4096 + qc_ * 512: p * 4096 + qc_ * 512 + 512],
                                 pv[0:64, :], rcp32[0:64, :], OP.mult, waits=[rc])
                        pvw[hh] = [cpm[hh], nv]
                    return go
                return [mk_cb(0), mk_cb(1), mk_v(0), mk_v(1)]

            def emit_pv_flat(f, vp=vp):
                qc_, g_ = divmod(f, 32)
                ps = f % 8
                for hh in range(2):
                    pv = pva if hh == 0 else pvb
                    w_ = [exps[f]] if hh == 0 else []
                    if g_ == 0:
                        w_ = w_ + pvw[hh]
                    mm(pv[0:65, :],
                       vp[:, g_ * 130 + hh * 65: g_ * 130 + (hh + 1) * 65],
                       pT_sl(ps)[:, hh * 512:(hh + 1) * 512],
                       g_ == 0, g_ == 31, waits=w_, inc=(hh == 1))
                pT_free[ps] = tE.at()
                del exps[f]
                if g_ == 31:
                    stage.extend(emit_norm(qc_))

            for gg in range(NTC * 32):
                qc, g = divmod(gg, 32)
                bcol = sm32[:, 104 + p * 8 + qc: 105 + p * 8 + qc]
                ab = gg % 2
                asl = arena[:, ab * 1024:(ab + 1) * 1024]
                for hh in range(2):
                    w_ = [arena_free[ab], ld] if hh == 0 else []
                    mm(asl[:, hh * 512:(hh + 1) * 512],
                       kp[hh * 64:(hh + 1) * 64, g * 128:(g + 1) * 128],
                       qp[hh * 64:(hh + 1) * 64, qc * 512:(qc + 1) * 512],
                       True, True, waits=w_, inc=(hh == 1))
                sc_done = tE.at()
                cnt = 0
                while next_pv <= gg - 4 and (cnt < 2 or next_pv <= gg - 7):
                    if next_pv % 32 == 0 and stage:
                        if next_pv <= gg - 7:
                            while stage:
                                stage.pop(0)()
                        else:
                            break
                    emit_pv_flat(next_pv)
                    next_pv += 1
                    cnt += 1
                ps = gg % 8
                w_ = [sc_done, pT_free[ps]]
                if first_exp:
                    w_.append(bias_done)
                    first_exp = False
                ex = act(pT_sl(ps), asl, AF.Exp, bias=bcol, scale=SCALE, waits=w_)
                arena_free[ab] = ex
                exps[gg] = ex
                if stage:
                    stage.pop(0)()
            while next_pv < NTC * 32:
                if next_pv % 32 == 0:
                    while stage:
                        stage.pop(0)()
                emit_pv_flat(next_pv)
                next_pv += 1
            while stage:
                stage.pop(0)()
            buf_free[buf] = [tE.at(), sE.at()]
        b1_done_v = vE.at()
        b1_done_t = tE.at()

        xq_freeB_l = [b1_done_t]
        prj_freeB = tern_tail_done
        arena_freeB = [arena_free[0], arena_free[0],
                       arena_free[1], arena_free[1]]
        pva_freeB = pvw[0][1]
        pvb_freeB = pvw[1][1]
        sqB_free = [a2_writes, a2_writes]
        sqB_free_t = b1_done_t
        x2_writes = None
        for qc in range(NTC):
            xl = []
            for dc in range(NDC):
                w_ = list(xq_freeB_l) if dc == 0 else []
                xl.append(dma("sync", xq[:, dc * 512:(dc + 1) * 512],
                              sx[qc * 512:(qc + 1) * 512, dc * 128:(dc + 1) * 128],
                              dL, waits=w_, transpose=True))
            smm = None
            for oc in range(NDC):
                sl = oc % 4
                aslice = arena[:, sl * 512:(sl + 1) * 512]
                for pc in range(NDC):
                    w_ = [arena_freeB[sl], wp_ld, b1_done_v] if pc == 0 else []
                    mmp = mm(aslice, wbig[:, pc * 1024 + oc * 128: pc * 1024 + (oc + 1) * 128],
                             big8[:, pc * 4096 + qc * 512: pc * 4096 + qc * 512 + 512],
                             pc == 0, pc == NDC - 1, waits=w_, inc=(pc == NDC - 1))
                ev = vop("tensor_scalar", prj[:, oc * 512:(oc + 1) * 512], aslice,
                         sm32[:, 80 + oc:81 + oc], None, OP.add,
                         waits=[mmp, prj_freeB])
                arena_freeB[sl] = ev
                sq = vop("tensor_tensor", sq16[:, (oc % 2) * 512:(oc % 2 + 1) * 512],
                         prj[:, oc * 512:(oc + 1) * 512], prj[:, oc * 512:(oc + 1) * 512],
                         OP.mult, waits=[ev, sqB_free[oc % 2], sqB_free_t])
                mm(pva[:, :], ones[:, :], prj[:, oc * 512:(oc + 1) * 512],
                   oc == 0, oc == NDC - 1,
                   waits=[ev] + ([pva_freeB] if oc == 0 else []))
                smm = mm(pvb[:, :], ones[:, :], sq16[:, (oc % 2) * 512:(oc % 2 + 1) * 512],
                         oc == 0, oc == NDC - 1,
                         waits=[sq] + ([pvb_freeB] if oc == 0 else []), inc=True)
                sqB_free_t = smm
            e1 = act(s_mean, pva[:, :], AF.Copy, scale=1.0 / C, waits=[smm])
            v1 = vop("tensor_tensor", s_t1, s_mean, s_mean, OP.mult, waits=[e1])
            v2 = vop("scalar_tensor_tensor", s_var, pvb[:, :], 1.0 / C, s_t1,
                     OP.mult, OP.subtract, waits=[v1])
            pva_freeB, pvb_freeB = e1, v2
            e2 = act(s_std, s_var, AF.Sqrt, bias=EPSC, waits=[v2])
            v3 = vop("reciprocal", s_a, s_std, waits=[e2])
            v4 = vop("scalar_tensor_tensor", s_b, s_mean, -1.0, s_a, OP.mult, OP.mult,
                     waits=[v3])
            last_v = None
            for oc in range(NDC):
                ob = oc % 2
                w_ = [v4, sqB_free_t] if oc == 0 else []
                vop("tensor_tensor", s_u, prj[:, oc * 512:(oc + 1) * 512], s_a,
                    OP.mult, waits=w_)
                vop("tensor_tensor", s_v, s_u, s_b, OP.add)
                x2s = vop("tensor_tensor", sq16[:, ob * 512:(ob + 1) * 512],
                          s_v, xq[:, oc * 512:(oc + 1) * 512], OP.add,
                          waits=[xl[oc], sqB_free[ob]])
                x2_writes = dma("gpsimd", sx2[oc * 128:(oc + 1) * 128, qc * 512:(qc + 1) * 512],
                                sq16[:, ob * 512:(ob + 1) * 512], dS, waits=[x2s])
                sqB_free[ob] = x2_writes
                last_v = x2s
            xq_freeB_l = [last_v]
            prj_freeB = last_v
            sqB_free_t = (tE, 0)
        b2_writes = dS.at()
        b2_done_s = sE.at()
        b2_done_v = vE.at()
        b2_done_t = tE.at()

        if "C" not in PH:
            return _finish()
        dno = kbuf[:, 4096:8192]

        def x2cb(i):
            return qbuf[:, 0:4096] if i % 2 == 0 else vbuf[:, 0:4096]

        def xnb(i):
            return kbuf[:, 0:4096] if i % 2 == 0 else vbuf[:, 4160:8256]

        def hb(i, fc):
            return big8[:, (i % 2) * 16384 + fc * 512:(i % 2) * 16384 + (fc + 1) * 512]

        st = {
            "x2_free": [[b1_done_t, (vE, 0)], [b1_done_t, (vE, 0)]],
            "xn_free": [b1_done_t, b1_done_t],
            "h_free": [b2_done_t, b2_done_t],
            "dno_free": (vE, 0),
            "pm_free": [[b2_done_s, b2_done_v], [b2_done_s, b2_done_v]],
            "arena01_free": [b2_done_s, b2_done_s],
            "outst_free": [(dO, 0), (dO, 0)],
            "xo_free": a1_done_v,
            "up_lw_free": [b2_done_v] * 4,
            "up_lw_mark": [None] * 4,
            "dn_lw_free": [b2_done_t, b2_done_t],
            "pv_free": [[b2_done_s, b2_done_v], [b2_done_s, b2_done_v]],
        }
        ln3_mark = {}
        ln4_state = {}

        def gen_ln3(i):
            lx2 = dma("sync", x2cb(i).rearrange("p (dc t) -> p dc t", t=512),
                      sx2[:, i * 512:(i + 1) * 512].rearrange("(dc p) t -> p dc t", p=128),
                      dL, waits=[b2_writes] + st["x2_free"][i % 2])
            yield
            x2c = x2cb(i)
            smm = None
            for dc in range(NDC):
                sq = vop("tensor_tensor", prj[:, (dc % 2) * 512:(dc % 2 + 1) * 512],
                         x2c[:, dc * 512:(dc + 1) * 512], x2c[:, dc * 512:(dc + 1) * 512],
                         OP.mult, waits=[lx2])
                mm(arena[:, 0:512], ones[:, :], x2c[:, dc * 512:(dc + 1) * 512],
                   dc == 0, dc == NDC - 1,
                   waits=[lx2] + ([st["arena01_free"][0]] if dc == 0 else []))
                smm = mm(arena[:, 512:1024], ones[:, :],
                         prj[:, (dc % 2) * 512:(dc % 2 + 1) * 512],
                         dc == 0, dc == NDC - 1,
                         waits=[sq] + ([st["arena01_free"][1]] if dc == 0 else []),
                         inc=True)
                yield
            e1 = act(s_mean, arena[:, 0:512], AF.Copy, scale=1.0 / C, waits=[smm])
            v1 = vop("tensor_tensor", s_t1, s_mean, s_mean, OP.mult, waits=[e1])
            v2 = vop("scalar_tensor_tensor", s_var, arena[:, 512:1024], 1.0 / C, s_t1,
                     OP.mult, OP.subtract, waits=[v1])
            st["arena01_free"] = [e1, v2]
            e2 = act(s_std, s_var, AF.Sqrt, bias=EPSC, waits=[v2])
            e3 = act(s_t1, s_var, AF.Sqrt, bias=ZERO, waits=[v2])
            v3 = vop("scalar_tensor_tensor", s_std, s_std, EPS_RMS, s_t1,
                     OP.mult, OP.add, waits=[e3, e2])
            v4 = vop("reciprocal", s_a, s_std, waits=[v3])
            v5 = vop("scalar_tensor_tensor", s_b, s_mean, -1.0, s_a, OP.mult, OP.mult,
                     waits=[v4])
            va = vop("tensor_copy", ab16[:, 0:512], s_a, waits=[v4])
            vb = vop("tensor_copy", ab16[:, 512:1024], s_b, waits=[v5])
            yield
            xn = xnb(i)
            xn_last = None
            for dc in range(NDC):
                w_ = [vb, st["xn_free"][i % 2]] if dc == 0 else []
                vop("tensor_tensor", prj[:, (dc % 2) * 512:(dc % 2 + 1) * 512],
                    x2c[:, dc * 512:(dc + 1) * 512], ab16[:, 0:512], OP.mult, waits=w_)
                xn_last = vop("tensor_tensor", xn[:, dc * 512:(dc + 1) * 512],
                              prj[:, (dc % 2) * 512:(dc % 2 + 1) * 512],
                              ab16[:, 512:1024], OP.add)
                yield
            ln3_mark[i] = xn_last

        def gen_ln4out(i):
            smm2, dn_last = ln4_state[i]
            e1 = act(s_mean, pm1[:, :], AF.Copy, scale=1.0 / C, waits=[smm2])
            v1 = vop("tensor_tensor", s_t1, s_mean, s_mean, OP.mult, waits=[e1])
            v2 = vop("scalar_tensor_tensor", s_var, pm2[:, :], 1.0 / C, s_t1,
                     OP.mult, OP.subtract, waits=[v1])
            st["pm_free"] = [[e1, v1], [v2, v2]]
            e2 = act(s_std, s_var, AF.Sqrt, bias=EPSC, waits=[v2])
            v3 = vop("reciprocal", s_a, s_std, waits=[e2])
            v4 = vop("scalar_tensor_tensor", s_b, s_mean, -1.0, s_a, OP.mult, OP.mult,
                     waits=[v3])
            yield
            xo_marks = []
            for dc in range(NDC):
                w_ = [v4] if dc > 0 else [v4, st["xo_free"]]
                vop("tensor_tensor", s_u, dno[:, dc * 512:(dc + 1) * 512], s_a,
                    OP.mult, waits=w_)
                vop("tensor_tensor", s_v, s_u, s_b, OP.add)
                xo = vop("tensor_tensor", xo32(dc), s_v,
                         x2cb(i)[:, dc * 512:(dc + 1) * 512], OP.add)
                xo_marks.append(xo)
                yield
            st["x2_free"][i % 2] = [xo_marks[-1]]
            st["dno_free"] = xo_marks[-1]
            prev_evs = None
            for tt in range(4):
                oh = tt % 2
                evs = []
                for dc in range(NDC):
                    pmt = pm1 if dc % 2 == 0 else pm2
                    if dc < 2:
                        if prev_evs is None:
                            w_ = [xo_marks[dc]] + st["pm_free"][dc]
                        else:
                            w_ = [xo_marks[dc], prev_evs[6 + dc]]
                    else:
                        w_ = [xo_marks[dc], evs[dc - 2]]

                    def tr(e, out=pmt[:, 0:128],
                           in_=xo32(dc)[:, tt * 128:tt * 128 + 128], ident=idf[:, :]):
                        return e.transpose(out, in_, ident)
                    em("tensor", tr, waits=w_, incs=[(tE, 1)])
                    trm = tE.at()
                    evv = vop("tensor_copy",
                              outst[:, oh * 1024 + dc * 128: oh * 1024 + (dc + 1) * 128],
                              pmt[:, 0:128], waits=[trm, st["outst_free"][oh]])
                    evs.append(evv)
                prev_evs = evs
                ow = dma("gpsimd", y[i * 512 + tt * 128: i * 512 + (tt + 1) * 128, :],
                         outst[:, oh * 1024:(oh + 1) * 1024], dO, waits=[evs[-1]])
                st["outst_free"][oh] = ow
                yield
            st["pm_free"] = [[(sE, 0), prev_evs[6]], [(sE, 0), prev_evs[7]]]
            st["xo_free"] = prev_evs[-1]

        def drain(g):
            if g is not None:
                while next(g, "d") != "d":
                    pass

        g3 = gen_ln3(0)
        drain(g3)
        g4 = None
        for qc in range(NTC):
            hsq = None
            pend_hsq = None

            def up_load(f):
                ub4 = f % 4
                st["up_lw_mark"][ub4] = dma(
                    "sync", xq[:, ub4 * 1024:(ub4 + 1) * 1024],
                    s1up[f * 128:(f + 1) * 128, :], dL,
                    waits=[tern_tail_done, st["up_lw_free"][ub4]])

            for f in range(3):
                up_load(f)
            for fc in range(NFC):
                ub4 = fc % 4
                lw = st["up_lw_mark"][ub4]
                sl = fc % 4
                aslice = arena[:, sl * 512:(sl + 1) * 512]
                for dc in range(NDC):
                    w_ = []
                    if dc == 0:
                        w_ = [lw, ln3_mark[qc]]
                        if fc < 2:
                            w_.append(st["arena01_free"][fc])
                        w_.append(st.get("gelu_free", {}).get(sl, (sE, 0)))
                    mmu = mm(aslice, xq[:, ub4 * 1024 + dc * 128: ub4 * 1024 + (dc + 1) * 128],
                             xnb(qc)[:, dc * 512:(dc + 1) * 512],
                             dc == 0, dc == NDC - 1, waits=w_, inc=(dc == NDC - 1))
                st["up_lw_free"][ub4] = mmu
                if fc + 3 < NFC:
                    up_load(fc + 3)
                if pend_hsq is not None:
                    hsq = pend_hsq()
                gl = act(hb(qc, fc), aslice, AF.Gelu,
                         waits=[mmu] + ([st["h_free"][qc % 2]] if fc == 0 else []))
                st.setdefault("gelu_free", {})[sl] = gl
                sqm = act(sq16[:, (fc % 2) * 512:(fc % 2 + 1) * 512], hb(qc, fc),
                          AF.Square,
                          waits=[st.get("sq16_free_t", (tE, 0))] if fc == 0 else [])

                def mk_hsq(fc=fc, sqm=sqm):
                    def go():
                        return mm(pva[:, :], ones[:, :],
                                  sq16[:, (fc % 2) * 512:(fc % 2 + 1) * 512],
                                  fc == 0, fc == NFC - 1,
                                  waits=[sqm] + (st["pv_free"][0] if fc == 0 else []),
                                  inc=(fc == NFC - 1))
                    return go
                pend_hsq = mk_hsq()
                if g4 is not None and fc >= 4 and fc % 2 == 0:
                    next(g4, None)
            hsq = pend_hsq()
            pend_hsq = None
            if g4 is not None:
                drain(g4)
            e4 = act(s_t1, pva[:, :], AF.Sqrt, bias=ZERO, scale=1.0 / DFF, waits=[hsq])
            v6 = vop("tensor_single_scalar", s_t1, s_t1, EPS_RMS, OP.add, waits=[e4])
            v7 = vop("reciprocal", s_u, s_t1, waits=[v6])
            gf = st.get("gelu_free", {})
            st["arena01_free"] = [gf.get(0, (sE, 0)), gf.get(1, (sE, 0))]
            g3 = gen_ln3(qc + 1) if qc + 1 < NTC else None
            smm2 = None
            dn_last = None
            sqd_free_t = hsq
            pend_pm = None
            for dc in range(NDC):
                db = dc % 2
                lw = dma("sync", wbig[:, db * 4096:(db + 1) * 4096],
                         s1dn[dc * 128:(dc + 1) * 128, :], dL,
                         waits=[tern_tail_done, st["dn_lw_free"][db]])
                pv = pva if db == 0 else pvb
                w0 = [lw, st["pv_free"][db][0], st["pv_free"][db][1]]
                if dc == 0:
                    w0.append(e4)
                for fc in range(NFC):
                    w_ = w0 if fc == 0 else []
                    mmd = mm(pv[:, :], wbig[:, db * 4096 + fc * 128: db * 4096 + (fc + 1) * 128],
                             big8[:, (qc % 2) * 16384 + fc * 512:(qc % 2) * 16384 + fc * 512 + 512],
                             fc == 0, fc == NFC - 1, waits=w_, inc=(fc == NFC - 1))
                st["dn_lw_free"][db] = mmd
                if pend_pm is not None:
                    sqd_free_t = pend_pm()
                ev = vop("tensor_tensor", dno[:, dc * 512:(dc + 1) * 512],
                         pv[:, :], s_u, OP.mult,
                         waits=[mmd, v7] + ([st["dno_free"]] if dc == 0 else []))
                st["pv_free"][db] = [ev, ev]
                dn_last = ev
                sq = vop("tensor_tensor", sq16[:, (dc % 2) * 512:(dc % 2 + 1) * 512],
                         dno[:, dc * 512:(dc + 1) * 512], dno[:, dc * 512:(dc + 1) * 512],
                         OP.mult, waits=[sqd_free_t])

                def mk_pm(dc=dc, ev=ev, sq=sq):
                    def go():
                        mm(pm1[:, :], ones[:, :], dno[:, dc * 512:(dc + 1) * 512],
                           dc == 0, dc == NDC - 1,
                           waits=[ev] + (st["pm_free"][0] if dc == 0 else []))
                        return mm(pm2[:, :], ones[:, :],
                                  sq16[:, (dc % 2) * 512:(dc % 2 + 1) * 512],
                                  dc == 0, dc == NDC - 1,
                                  waits=[sq] + (st["pm_free"][1] if dc == 0 else []),
                                  inc=True)
                    return go
                pend_pm = mk_pm()
                if g3 is not None:
                    next(g3, None)
                    next(g3, None)
            smm2 = pend_pm()
            st["h_free"][qc % 2] = smm2
            st["sq16_free_t"] = smm2
            st["xn_free"][qc % 2] = st["up_lw_free"][(NFC - 1) % 2]
            ln4_state[qc] = (smm2, dn_last)
            drain(g3)
            g4 = gen_ln4out(qc)
        drain(g4)

        return _finish()

        return _finish()


def make_consts():
    ROPE_BASE, ROPE_SCALE = 10000.0, 3.0
    j = np.arange(0, D, 2, dtype=np.float32)
    theta = (1.0 / ROPE_BASE ** (j / D)) * ROPE_SCALE
    t = np.arange(L, dtype=np.float32)
    ang = t[None, :] * theta[:, None]
    cosv, sinv = np.cos(ang), np.sin(ang)
    ctab = np.zeros((128, L), np.float32)
    stab = np.zeros((128, L), np.float32)
    for p in range(128):
        jj = (p % D) // 2
        ctab[p] = cosv[jj]
        stab[p] = sinv[jj] * (-1.0 if p % 2 == 0 else 1.0)
    swp = np.zeros((128, 128), np.float32)
    for k in range(128):
        swp[k, k ^ 1] = 1.0
    blkones = np.zeros((128, 128), np.float32)
    blkones[0:64, 0:64] = 1.0
    blkones[64:128, 64:128] = 1.0
    cvals = np.zeros((128, 8), np.float32)
    cvals[:, 1] = EPS_LN
    bf = ml_dtypes.bfloat16
    return dict(ctab=ctab.astype(bf), stab=stab.astype(bf), swp=swp.astype(bf),
                ones128=np.ones((128, 128), bf), blkones=blkones.astype(bf),
                onesf=np.ones((1, 128), np.float32),
                identf=np.eye(128, dtype=np.float32),
                ones32=np.ones((128, 128), np.float32), cvals=cvals)


def kernel(L0_attn_x, L0_attn_w_qkv, L0_attn_w_proj, L0_attn_b_proj,
           L0_ffn_w_up, L0_ffn_w_down):
    xs = np.ascontiguousarray(np.asarray(L0_attn_x, dtype=np.float32))
    wqkv = np.ascontiguousarray(np.asarray(L0_attn_w_qkv, dtype=np.float32))
    wproj = np.ascontiguousarray(np.asarray(L0_attn_w_proj, dtype=np.float32))
    bproj = np.asarray(L0_attn_b_proj, dtype=np.float32)
    wup = np.ascontiguousarray(np.asarray(L0_ffn_w_up, dtype=np.float32))
    wdown = np.ascontiguousarray(np.asarray(L0_ffn_w_down, dtype=np.float32))
    bprojT = np.ascontiguousarray(bproj.reshape(NDC, 128).T)

    consts = make_consts()
    nc = build_nc()
    in_maps = []
    for i in range(8):
        m = dict(x=np.ascontiguousarray(xs[i]), wqkv=wqkv, wproj=wproj,
                 bprojT=bprojT, wup=wup, wdown=wdown)
        m.update(consts)
        in_maps.append(m)
    res = run_bass_kernel_spmd(nc, in_maps, core_ids=list(range(8)))
    out = np.stack([np.asarray(r["y"], dtype=np.float32) for r in res.results], axis=0)
    return out
